# version 01
# baseline (speedup 1.0000x reference)
import math
from contextlib import ExitStack

import numpy as np

import concourse.bass as bass
import concourse.mybir as mybir
from concourse.bass_utils import run_bass_kernel_spmd

F32 = mybir.dt.float32
BF16 = mybir.dt.bfloat16
AF = mybir.ActivationFunctionType
ALU = mybir.AluOpType
AX = mybir.AxisListType

N_CORES = 8
B_PER_CORE = 2
SEQ = 2048
DM = 1024
NT = SEQ // 128

SQ32 = 1.0 / math.sqrt(32.0)
PHI2C = 1.0 / (math.sqrt(2.0) * 4.0)

ENGINES = ("sync", "scalar", "vector", "tensor", "gpsimd")


class Sem:
    __slots__ = ("name", "n", "h")

    def __init__(self, name):
        self.name = name
        self.n = 0
        self.h = None


class Tok:
    __slots__ = ("sem", "val")

    def __init__(self, sem, val):
        self.sem = sem
        self.val = val


class B:
    def __init__(self):
        self.ops = {e: [] for e in ENGINES}
        self.sems = []
        self.esem = {e: self._sem("E" + e[:2]) for e in ENGINES}
        self.last = {e: None for e in ENGINES}

    def _sem(self, name):
        s = Sem(name)
        self.sems.append(s)
        return s

    def sem(self, name):
        return self._sem(name)

    def emit(self, eng, fn, deps=(), dma=False, sem=None, chain=None):
        if chain is None:
            chain = eng in ("scalar", "vector", "gpsimd")
        s = sem if sem is not None else self.esem[eng]
        amt = 16 if dma else 1
        d = []
        for x in deps:
            if x is None:
                continue
            if isinstance(x, (list, tuple)):
                d.extend(y for y in x if y is not None)
            else:
                d.append(x)
        if chain and self.last[eng] is not None:
            d.append(self.last[eng])
        s.n += amt
        tok = Tok(s, s.n)
        self.ops[eng].append((fn, tuple(d), s, amt))
        self.last[eng] = tok
        return tok

    def build(self, nc, st):
        for s in self.sems:
            s.h = st.enter_context(nc.semaphore(s.name))
        block = st.enter_context(nc.Block())
        for e in ENGINES:
            if self.ops[e]:
                self._section(block, e, self.ops[e])

    @staticmethod
    def _section(block, e, ops):
        def body(engobj):
            hwm = {}
            for fn, deps, s, amt in ops:
                for tk in deps:
                    if hwm.get(id(tk.sem), -1) < tk.val:
                        engobj.wait_ge(tk.sem.h, tk.val)
                        hwm[id(tk.sem)] = tk.val
                instr = fn(engobj)
                instr.then_inc(s.h, amt)

        getattr(block, e)(body)


def make_consts():
    c = {}
    c["c_ident"] = np.eye(128, dtype=np.float32)
    c["c_identf"] = np.eye(128, dtype=np.float32)
    c["c_ones"] = np.ones((128, 1), dtype=np.float32)
    jj, ii = np.meshgrid(np.arange(128), np.arange(128), indexing="ij")
    c["c_trilkm"] = (jj <= ii).astype(np.float32)
    selA = np.zeros((128, 8, 256), dtype=np.float32)
    selB = np.zeros((128, 8, 256), dtype=np.float32)
    selLow = np.zeros((128, 8, 16), dtype=np.float32)
    hm = np.zeros((128, 8), dtype=np.float32)
    for h in range(8):
        for ab in range(256):
            a, bb = ab // 16, ab % 16
            selA[16 * h + a, h, ab] = 1.0
            selB[16 * h + bb, h, ab] = 1.0
        for f in range(16):
            selLow[16 * h + f, h, f] = 0.5
        hm[16 * h:16 * h + 16, h] = 1.0
    c["c_selA"] = selA
    c["c_selB"] = selB
    c["c_selLow"] = selLow
    c["c_headmask"] = hm
    return c


CONST_SPECS = {
    "c_ident": ([128, 128], BF16),
    "c_identf": ([128, 128], F32),
    "c_ones": ([128, 1], BF16),
    "c_trilkm": ([128, 128], F32),
    "c_selA": ([128, 8, 256], BF16),
    "c_selB": ([128, 8, 256], BF16),
    "c_selLow": ([128, 8, 16], BF16),
    "c_headmask": ([128, 8], F32),
}


def emit_const_loads(b, nc, st, t, names):
    for nm in names:
        shape, dt = CONST_SPECS[nm]
        t[nm] = st.enter_context(nc.sbuf_tensor("sb_" + nm, shape, dt))
        ap = nc.dram_tensor(nm, shape, F32, kind="ExternalInput").ap()
        t["tok_" + nm] = b.emit(
            "gpsimd", lambda e, nm=nm, ap=ap: e.dma_start(out=t[nm][...], in_=ap),
            dma=True, sem=t["sem_wld"])


def emit_weight_load(b, nc, st, t, name, dram_ap, kchunks, ncols, dt=BF16):
    t[name] = st.enter_context(nc.sbuf_tensor(name, [128, kchunks, ncols], dt))
    t["tok_" + name] = b.emit(
        "gpsimd", lambda e: e.dma_start(
            out=t[name][:, :, :],
            in_=dram_ap.rearrange("(k p) n -> p k n", p=128)),
        dma=True, sem=t["sem_wld"])


def emit_ln(b, t, pref, src_ap, dst_ap, W, deps, g_ap=None, bia_ap=None,
            extra_dst=None, dep_extra=None):
    red = b.emit("vector", lambda e: e.reduce_sum(
        t[pref + "_m"][:, :], src_ap, axis=AX.X),
        deps=list(deps) + [t.get(pref + "_lnbusy")])
    sq = b.emit("scalar", lambda e: e.activation(
        t[pref + "_sq"][:, 0:W], src_ap, AF.Square,
        accum_out=t[pref + "_v"][:, :]),
        deps=list(deps) + [t.get(pref + "_lnbusyv")])
    mm = b.emit("vector", lambda e: e.tensor_scalar_mul(
        t[pref + "_m"][:, :], t[pref + "_m"][:, :], 1.0 / W), deps=[red])
    msq = b.emit("scalar", lambda e: e.activation(
        t[pref + "_m2"][:, :], t[pref + "_m"][:, :], AF.Square), deps=[mm, sq])
    vv = b.emit("vector", lambda e: e.scalar_tensor_tensor(
        t[pref + "_v"][:, :], t[pref + "_v"][:, :], 1.0 / W,
        t[pref + "_m2"][:, :], op0=ALU.mult, op1=ALU.subtract), deps=[msq])
    vv = b.emit("vector", lambda e: e.tensor_scalar(
        t[pref + "_v"][:, :], t[pref + "_v"][:, :], 1e-5, None, op0=ALU.add),
        deps=[vv])
    sqr = b.emit("scalar", lambda e: e.activation(
        t[pref + "_s"][:, :], t[pref + "_v"][:, :], AF.Sqrt), deps=[vv])
    rc = b.emit("vector", lambda e: e.reciprocal(
        t[pref + "_r"][:, :], t[pref + "_s"][:, :]), deps=[sqr])
    t[pref + "_lnbusy"] = sqr
    t[pref + "_lnbusyv"] = rc
    out = b.emit("vector", lambda e: e.tensor_scalar(
        dst_ap, src_ap, t[pref + "_m"][:, :], t[pref + "_r"][:, :],
        op0=ALU.subtract, op1=ALU.mult), deps=[rc])
    if g_ap is not None:
        out = b.emit("vector", lambda e: e.tensor_tensor(
            dst_ap, dst_ap, g_ap, op=ALU.mult),
            deps=[out] + (dep_extra or []))
    if bia_ap is not None:
        out = b.emit("vector", lambda e: e.tensor_tensor(
            dst_ap, dst_ap, bia_ap, op=ALU.add), deps=[out])
    if extra_dst is not None:
        out = b.emit("vector", lambda e: e.tensor_copy(extra_dst, dst_ap),
                     deps=[out])
    return out


def alloc_ln(st, nc, t, pref, W=DM):
    t[pref + "_sq"] = st.enter_context(nc.sbuf_tensor(pref + "_sq", [128, W], F32))
    for nm in ("_m", "_v", "_m2", "_s", "_r"):
        t[pref + nm] = st.enter_context(nc.sbuf_tensor(pref + nm, [128, 1], F32))


def emit_pre(b, nc, t, x_dram, h_dram, hT_dram, want_h, want_hT, hTf_dram=None):
    last_store = None
    for s in range(B_PER_CORE):
        for i in range(NT):
            p = (s * NT + i) % 2
            ld = b.emit("sync", lambda e, s=s, i=i, p=p: e.dma_start(
                out=t["pre_x%d" % p][:, :], in_=x_dram[s, i * 128:(i + 1) * 128, :]),
                deps=[t.get("pre_x%d_free" % p)], dma=True, sem=t["sem_pre_ld%d" % p])
            h = emit_ln(b, t, "po", t["pre_x%d" % p][:, :], t["pre_h%d" % p][:, :],
                        DM, [ld, t.get("pre_h%d_free" % p)])
            t["pre_x%d_free" % p] = h
            st_toks = []
            if want_h:
                st_toks.append(b.emit("sync", lambda e, s=s, i=i, p=p: e.dma_start(
                    out=h_dram[s, i * 128:(i + 1) * 128, :], in_=t["pre_h%d" % p][:, :]),
                    deps=[h], dma=True, sem=t["sem_pre_st%d" % p]))
            if want_hT:
                hb = b.emit("vector", lambda e, p=p: e.tensor_copy(
                    t["pre_hbf"][:, :], t["pre_h%d" % p][:, :]),
                    deps=[h, t.get("pre_hbf_free")])
                cps = []
                for k in range(8):
                    tr = b.emit("tensor", lambda e, k=k, p=p: e.transpose(
                        t["pre_psT"][:, :], t["pre_hbf"][:, k * 128:(k + 1) * 128],
                        t["c_ident"][:, :]),
                        deps=[hb, t.get("pre_psT_free"), t["tok_c_ident"]])
                    cp = b.emit("vector", lambda e, k=k: e.tensor_copy(
                        t["pre_hT"][:, k, :], t["pre_psT"][:, :]),
                        deps=[tr, t.get("pre_hT_free")])
                    t["pre_psT_free"] = cp
                    cps.append(cp)
                t["pre_hbf_free"] = cps[-1]
                sttok = b.emit("sync", lambda e, s=s, i=i: e.dma_start(
                    out=hT_dram[s].rearrange("(k p) c -> p k c", p=128)[:, :, i * 128:(i + 1) * 128],
                    in_=t["pre_hT"][:, :, :]),
                    deps=[cps[-1]], dma=True, sem=t["sem_pre_st%d" % p])
                st_toks.append(sttok)
                if hTf_dram is not None:
                    fc = b.emit("vector", lambda e: e.tensor_copy(
                        t["pre_hTf"][:, :, :], t["pre_hT"][:, :, ::-1]),
                        deps=[cps[-1], t.get("pre_hTf_free")])
                    sttok2 = b.emit("sync", lambda e, s=s, i=i: e.dma_start(
                        out=hTf_dram[s].rearrange("(k p) c -> p k c", p=128)[:, :, (NT - 1 - i) * 128:(NT - i) * 128],
                        in_=t["pre_hTf"][:, :, :]),
                        deps=[fc], dma=True, sem=t["sem_pre_st%d" % p])
                    t["pre_hTf_free"] = sttok2
                    st_toks.append(sttok2)
            full = Tok(t["sem_pre_st%d" % p], t["sem_pre_st%d" % p].n)
            if want_hT:
                t["pre_hT_free"] = [full, fc] if hTf_dram is not None else full
                if hTf_dram is not None:
                    t["pre_hTf_free"] = full
            t["pre_h%d_free" % p] = full if st_toks else h
            last_store = full if st_toks else last_store
    return [Tok(t["sem_pre_st0"], t["sem_pre_st0"].n),
            Tok(t["sem_pre_st1"], t["sem_pre_st1"].n)]


def alloc_pre(st, nc, b, t, want_hT=True, want_hTf=False):
    for p in range(2):
        t["pre_x%d" % p] = st.enter_context(nc.sbuf_tensor("pre_x%d" % p, [128, DM], F32))
        t["pre_h%d" % p] = st.enter_context(nc.sbuf_tensor("pre_h%d" % p, [128, DM], F32))
        t["sem_pre_ld%d" % p] = b.sem("pre_ld%d" % p)
        t["sem_pre_st%d" % p] = b.sem("pre_st%d" % p)
    alloc_ln(st, nc, t, "po")
    if want_hT:
        t["pre_hbf"] = st.enter_context(nc.sbuf_tensor("pre_hbf", [128, DM], BF16))
        t["pre_hT"] = st.enter_context(nc.sbuf_tensor("pre_hT", [128, 8, 128], BF16))
        t["pre_psT"] = st.enter_context(nc.psum_tensor("pre_psT", [128, 128], BF16))
    if want_hTf:
        t["pre_hTf"] = st.enter_context(nc.sbuf_tensor("pre_hTf", [128, 8, 128], BF16))


def emit_post(b, nc, t, x_dram, y_dram, xo_dram):
    last = None
    for s in range(B_PER_CORE):
        for i in range(NT):
            p = (s * NT + i) % 2
            ldy = b.emit("sync", lambda e, s=s, i=i, p=p: e.dma_start(
                out=t["po_y%d" % p][:, :], in_=y_dram[s, i * 128:(i + 1) * 128, :]),
                deps=[t.get("po_y%d_free" % p)], dma=True, sem=t["sem_po_ld%d" % p])
            ldx = b.emit("sync", lambda e, s=s, i=i, p=p: e.dma_start(
                out=t["po_x%d" % p][:, :], in_=x_dram[s, i * 128:(i + 1) * 128, :]),
                deps=[t.get("po_x%d_free" % p)], dma=True, sem=t["sem_po_ldx%d" % p])
            ln = emit_ln(b, t, "po", t["po_y%d" % p][:, :], t["po_y%d" % p][:, :],
                         DM, [ldy])
            add = b.emit("vector", lambda e, p=p: e.tensor_tensor(
                t["po_y%d" % p][:, :], t["po_y%d" % p][:, :], t["po_x%d" % p][:, :],
                op=ALU.add), deps=[ln, ldx])
            t["po_x%d_free" % p] = add
            sttok = b.emit("sync", lambda e, s=s, i=i, p=p: e.dma_start(
                out=xo_dram[s, i * 128:(i + 1) * 128, :], in_=t["po_y%d" % p][:, :]),
                deps=[add], dma=True, sem=t["sem_po_st%d" % p])
            t["po_y%d_free" % p] = sttok
            last = sttok
    return last


def alloc_post(st, nc, b, t):
    for p in range(2):
        t["po_y%d" % p] = t["pre_x%d" % p]
        t["po_x%d" % p] = t["pre_h%d" % p]
        t["sem_po_ld%d" % p] = b.sem("po_ld%d" % p)
        t["sem_po_ldx%d" % p] = b.sem("po_ldx%d" % p)
        t["sem_po_st%d" % p] = b.sem("po_st%d" % p)
    if "po_sq" not in t:
        alloc_ln(st, nc, t, "po")


def emit_l0_attn(b, nc, t, hT_dram, y_dram):
    for s in range(B_PER_CORE):
        ldh = b.emit("sync", lambda e, s=s: e.dma_start(
            out=t["a_hT"][:, :, :],
            in_=hT_dram[s].rearrange("(k p) c -> p k c", p=128)),
            deps=[t.get("a_seq_done")], dma=True, sem=t["sem_a_ld"])
        for wname, dst in (("a_wq", "a_qfm"), ("a_wk", "a_kfm")):
            for mc in range(2):
                for ncol in range(SEQ // 512):
                    mmtok = None
                    for kc in range(8):
                        mmtok = b.emit("tensor", lambda e, wname=wname, mc=mc, ncol=ncol, kc=kc: e.matmul(
                            t["a_ps0"][:, 0:512],
                            t[wname][:, kc, mc * 128:(mc + 1) * 128],
                            t["a_hT"][:, kc, ncol * 512:(ncol + 1) * 512],
                            start=(kc == 0), stop=(kc == 7)),
                            deps=[ldh, t["tok_" + wname], t.get("a_ps0_free")])
                    t["a_ps0_free"] = b.emit("vector", lambda e, dst=dst, mc=mc, ncol=ncol: e.tensor_copy(
                        t[dst][:, mc, ncol * 512:(ncol + 1) * 512], t["a_ps0"][:, 0:512]),
                        deps=[mmtok])
        for i in range(NT):
            for ncol in range(2):
                mmtok = None
                for kc in range(8):
                    mmtok = b.emit("tensor", lambda e, i=i, ncol=ncol, kc=kc: e.matmul(
                        t["a_ps0"][:, 0:512],
                        t["a_hT"][:, kc, i * 128:(i + 1) * 128],
                        t["a_wv"][:, kc, ncol * 512:(ncol + 1) * 512],
                        start=(kc == 0), stop=(kc == 7)),
                        deps=[ldh, t["tok_a_wv"], t.get("a_ps0_free")])
                t["a_ps0_free"] = b.emit("vector", lambda e, i=i, ncol=ncol: e.tensor_copy(
                    t["a_vtm"][:, i, ncol * 512:(ncol + 1) * 512], t["a_ps0"][:, 0:512]),
                    deps=[mmtok])

        for c in range(NT):
            head_done = []
            for hd in range(16):
                tile = hd // 8
                hh = hd % 8
                km = b.emit("vector", lambda e, tile=tile, hh=hh, c=c: e.tensor_single_scalar(
                    t["a_kmask"][:, :], t["a_kfm"][:, tile, c * 128:(c + 1) * 128],
                    t["c_headmask"][:, hh:hh + 1], op=ALU.mult),
                    deps=[t.get("a_kmask_free"), t["tok_c_headmask"]])
                zmm = b.emit("tensor", lambda e, tile=tile, c=c: e.matmul(
                    t["a_psz"][:, 0, :], t["a_kmask"][:, :],
                    t["a_qfm"][:, tile, c * 128:(c + 1) * 128],
                    start=True, stop=True),
                    deps=[km, t.get("a_psz0_free"), t.get("a_psz1_free")])
                t["a_kmask_free"] = zmm
                sq = b.emit("scalar", lambda e: e.activation(
                    t["a_tq"][:, :], t["a_psz"][:, 0, :], AF.Square, scale=SQ32),
                    deps=[zmm, t.get("a_tq_free")])
                s1 = b.emit("vector", lambda e: e.scalar_tensor_tensor(
                    t["a_s1"][:, :], t["a_psz"][:, 0, :], 0.25, t["a_tq"][:, :],
                    op0=ALU.mult, op1=ALU.add), deps=[sq, zmm])
                t["a_psz0_free"] = s1
                t["a_tq_free"] = s1
                sbf = b.emit("vector", lambda e: e.scalar_tensor_tensor(
                    t["a_sbf"][:, :], t["a_s1"][:, :], 1.0, t["c_trilkm"][:, :],
                    op0=ALU.add, op1=ALU.mult),
                    deps=[s1, t.get("a_sbf_free"), t["tok_c_trilkm"]])
                phq = []
                for abc in range(2):
                    ra = b.emit("tensor", lambda e, hh=hh, abc=abc, tile=tile, c=c: e.matmul(
                        t["a_psra"][:, 0:128], t["c_selA"][:, hh, abc * 128:(abc + 1) * 128],
                        t["a_qfm"][:, tile, c * 128:(c + 1) * 128], start=True, stop=True),
                        deps=[t.get("a_psra_free"), t["tok_c_selA"]])
                    rb = b.emit("tensor", lambda e, hh=hh, abc=abc, tile=tile, c=c: e.matmul(
                        t["a_psrb"][:, 0:128], t["c_selB"][:, hh, abc * 128:(abc + 1) * 128],
                        t["a_qfm"][:, tile, c * 128:(c + 1) * 128], start=True, stop=True),
                        deps=[t.get("a_psrb_free"), t["tok_c_selB"]])
                    ev = b.emit("scalar", lambda e: e.copy(
                        t["a_ra"][:, 0:128], t["a_psra"][:, 0:128]),
                        deps=[ra, t.get("a_ra_free")])
                    t["a_psra_free"] = ev
                    ph = b.emit("vector", lambda e, abc=abc: e.scalar_tensor_tensor(
                        t["a_phq"][:, abc, :], t["a_ra"][:, 0:128], PHI2C,
                        t["a_psrb"][:, 0:128], op0=ALU.mult, op1=ALU.mult),
                        deps=[ev, rb, t.get("a_phq_free")])
                    t["a_psrb_free"] = ph
                    t["a_ra_free"] = ph
                    phq.append(ph)
                rl = b.emit("tensor", lambda e, hh=hh, tile=tile, c=c: e.matmul(
                    t["a_psra"][0:16, 0:128], t["c_selLow"][:, hh, :],
                    t["a_qfm"][:, tile, c * 128:(c + 1) * 128], start=True, stop=True),
                    deps=[t.get("a_psra_free"), t["tok_c_selLow"]])
                phl = b.emit("vector", lambda e: e.tensor_copy(
                    t["a_phq"][0:16, 2, :], t["a_psra"][0:16, 0:128]),
                    deps=[rl, t.get("a_phq_free")])
                t["a_psra_free"] = phl
                phq.append(phl)
                ra = b.emit("tensor", lambda e, hh=hh, tile=tile, c=c: e.matmul(
                    t["a_psra"][:, 0:256], t["a_kfm"][:, tile, c * 128:(c + 1) * 128],
                    t["c_selA"][:, hh, :], start=True, stop=True),
                    deps=[t.get("a_psra_free")])
                rb = b.emit("tensor", lambda e, hh=hh, tile=tile, c=c: e.matmul(
                    t["a_psrb"][:, 0:256], t["a_kfm"][:, tile, c * 128:(c + 1) * 128],
                    t["c_selB"][:, hh, :], start=True, stop=True),
                    deps=[t.get("a_psrb_free")])
                ev = b.emit("scalar", lambda e: e.copy(
                    t["a_rak"][:, :], t["a_psra"][:, 0:256]),
                    deps=[ra, t.get("a_rak_free")])
                t["a_psra_free"] = ev
                pk1 = b.emit("vector", lambda e: e.scalar_tensor_tensor(
                    t["a_phk"][:, 0:256], t["a_rak"][:, :], PHI2C,
                    t["a_psrb"][:, 0:256], op0=ALU.mult, op1=ALU.mult),
                    deps=[ev, rb, t.get("a_phk_free")])
                t["a_psrb_free"] = pk1
                t["a_rak_free"] = pk1
                rl = b.emit("tensor", lambda e, hh=hh, tile=tile, c=c: e.matmul(
                    t["a_psra"][:, 0:16], t["a_kfm"][:, tile, c * 128:(c + 1) * 128],
                    t["c_selLow"][:, hh, :], start=True, stop=True),
                    deps=[t.get("a_psra_free")])
                pk2 = b.emit("vector", lambda e: e.tensor_copy(
                    t["a_phk"][:, 256:272], t["a_psra"][:, 0:16]), deps=[rl])
                t["a_psra_free"] = pk2
                nmm = b.emit("tensor", lambda e, hd=hd, c=c: e.matmul(
                    t["a_psout"][:, 0:64], t["a_sbf"][:, :],
                    t["a_vtm"][:, c, hd * 64:(hd + 1) * 64],
                    start=True, stop=(c == 0)),
                    deps=[sbf, t.get("a_psout_free")])
                if c > 0:
                    for fc in range(3):
                        wid = 128 if fc < 2 else 17
                        nmm = b.emit("tensor", lambda e, fc=fc, wid=wid, hd=hd: e.matmul(
                            t["a_psout"][:, 0:64],
                            t["a_phq"][0:wid, fc, :],
                            t["a_Sbf"][0:wid, hd, fc * 64:(fc + 1) * 64],
                            start=False, stop=(fc == 2)),
                            deps=[phq[fc], t["tok_Sbf", hd]])
                dmm = b.emit("tensor", lambda e, c=c: e.matmul(
                    t["a_psden"][:, 0:1], t["a_sbf"][:, :], t["c_ones"][:, :],
                    start=True, stop=(c == 0)),
                    deps=[sbf, t.get("a_psden_free"), t["tok_c_ones"]])
                t["a_sbf_free"] = dmm
                if c > 0:
                    for fc in range(3):
                        wid = 128 if fc < 2 else 17
                        dmm = b.emit("tensor", lambda e, fc=fc, wid=wid, hd=hd: e.matmul(
                            t["a_psden"][:, 0:1],
                            t["a_phq"][0:wid, fc, :],
                            t["a_Sbf"][0:wid, hd, 192 + fc:193 + fc],
                            start=False, stop=(fc == 2)),
                            deps=[phq[fc], t["tok_Sbf", hd]])
                de = b.emit("vector", lambda e: e.tensor_scalar(
                    t["a_den"][:, :], t["a_psden"][:, 0:1], 1e-9, None, op0=ALU.add),
                    deps=[dmm])
                t["a_psden_free"] = de
                rc = b.emit("vector", lambda e: e.reciprocal(
                    t["a_rden"][:, :], t["a_den"][:, :]), deps=[de])
                dv = b.emit("vector", lambda e, hd=hd: e.tensor_single_scalar(
                    t["a_outall"][:, hd * 64:(hd + 1) * 64], t["a_psout"][:, 0:64],
                    t["a_rden"][:, :], op=ALU.mult),
                    deps=[rc, nmm, t.get("a_outall_free")])
                t["a_psout_free"] = dv
                head_done.append(dv)
                up = None
                for fc in range(3):
                    up = b.emit("tensor", lambda e, fc=fc, hd=hd, c=c: e.matmul(
                        t["a_psS"][:, fc * 64:(fc + 1) * 64],
                        t["a_phk"][:, fc * 128:(fc + 1) * 128],
                        t["a_vtm"][:, c, hd * 64:(hd + 1) * 64],
                        start=(fc == 0), stop=False),
                        deps=[pk1, pk2, t.get("a_psS_free")])
                for fc in range(3):
                    up = b.emit("tensor", lambda e, fc=fc: e.matmul(
                        t["a_psS"][:, 192 + fc:193 + fc],
                        t["a_phk"][:, fc * 128:(fc + 1) * 128],
                        t["c_ones"][:, :],
                        start=False, stop=(fc == 2)),
                        deps=[pk1, pk2])
                t["a_phk_free"] = up
                t["a_phq_free"] = dmm
                if c == 0:
                    acc = b.emit("vector", lambda e, hd=hd: e.tensor_copy(
                        t["a_Sf"][:, hd, :], t["a_psS"][:, 0:195]), deps=[up])
                else:
                    acc = b.emit("vector", lambda e, hd=hd: e.tensor_tensor(
                        t["a_Sf"][:, hd, :], t["a_Sf"][:, hd, :], t["a_psS"][:, 0:195],
                        op=ALU.add), deps=[up, t["tok_Sf", hd]])
                t["a_psS_free"] = acc
                t["tok_Sf", hd] = acc
                cpb = b.emit("vector", lambda e, hd=hd: e.tensor_copy(
                    t["a_Sbf"][:, hd, :], t["a_Sf"][:, hd, :]),
                    deps=[acc, (dmm if c > 0 else None)])
                t["tok_Sbf", hd] = cpb
            trtoks = []
            for k in range(8):
                tr = b.emit("tensor", lambda e, k=k: e.transpose(
                    t["a_psz"][:, 1, :],
                    t["a_outall"][:, k * 128:(k + 1) * 128], t["c_identf"][:, :]),
                    deps=head_done + [t.get("a_psz1_free"), t["tok_c_identf"]])
                cp = b.emit("vector", lambda e, k=k: e.tensor_copy(
                    t["a_outT"][:, k, :], t["a_psz"][:, 1, :]),
                    deps=[tr, t.get("a_outT_free")])
                t["a_psz1_free"] = cp
                trtoks.append(cp)
            t["a_outall_free"] = trtoks[-1]
            for ncol in range(2):
                mmtok = None
                for kc in range(8):
                    mmtok = b.emit("tensor", lambda e, ncol=ncol, kc=kc: e.matmul(
                        t["a_ps0"][:, 0:512], t["a_outT"][:, kc, :],
                        t["a_wo"][:, kc, ncol * 512:(ncol + 1) * 512],
                        start=(kc == 0), stop=(kc == 7)),
                        deps=[trtoks[kc], t["tok_a_wo"], t.get("a_ps0_free")])
                ev = b.emit("vector", lambda e, ncol=ncol: e.tensor_copy(
                    t["a_y"][:, ncol * 512:(ncol + 1) * 512], t["a_ps0"][:, 0:512]),
                    deps=[mmtok, t.get("a_y_free")])
                t["a_ps0_free"] = ev
            t["a_outT_free"] = mmtok
            sttok = b.emit("sync", lambda e, s=s, c=c: e.dma_start(
                out=y_dram[s, c * 128:(c + 1) * 128, :], in_=t["a_y"][:, :]),
                deps=[t["a_ps0_free"]], dma=True, sem=t["sem_a_st"])
            t["a_y_free"] = sttok
            t["a_seq_done"] = sttok
    return t["a_seq_done"]


def alloc_l0_attn(st, nc, b, t):
    sb = lambda nm, shape, dt: st.enter_context(nc.sbuf_tensor(nm, shape, dt))
    t["a_hT"] = sb("a_hT", [128, 8, SEQ], BF16)
    t["a_qfm"] = sb("a_qfm", [128, 2, SEQ], BF16)
    t["a_kfm"] = sb("a_kfm", [128, 2, SEQ], BF16)
    t["a_vtm"] = sb("a_vtm", [128, NT, 1024], BF16)
    t["a_kmask"] = sb("a_kmask", [128, 128], BF16)
    t["a_tq"] = sb("a_tq", [128, 128], F32)
    t["a_s1"] = sb("a_s1", [128, 128], F32)
    t["a_sbf"] = sb("a_sbf", [128, 128], BF16)
    t["a_ra"] = sb("a_ra", [128, 128], F32)
    t["a_rak"] = sb("a_rak", [128, 256], F32)
    t["a_phq"] = sb("a_phq", [128, 3, 128], BF16)
    t["a_phk"] = sb("a_phk", [128, 384], BF16)
    t["a_Sf"] = sb("a_Sf", [128, 16, 195], F32)
    t["a_Sbf"] = sb("a_Sbf", [128, 16, 195], BF16)
    t["a_den"] = sb("a_den", [128, 1], F32)
    t["a_rden"] = sb("a_rden", [128, 1], F32)
    t["a_outall"] = sb("a_outall", [128, 1024], F32)
    t["a_outT"] = sb("a_outT", [128, 8, 128], BF16)
    t["a_y"] = sb("a_y", [128, 1024], F32)
    ps = lambda nm, shape, dt: st.enter_context(nc.psum_tensor(nm, shape, dt))
    t["a_ps0"] = ps("a_ps0", [128, 512], F32)
    t["a_psz"] = ps("a_psz", [128, 2, 128], F32)
    t["a_psra"] = ps("a_psra", [128, 256], F32)
    t["a_psrb"] = ps("a_psrb", [128, 256], F32)
    t["a_psout"] = ps("a_psout", [128, 64], F32)
    t["a_psden"] = ps("a_psden", [128, 4], F32)
    t["a_psS"] = ps("a_psS", [128, 195], F32)
    t["sem_a_ld"] = b.sem("a_ld")
    t["sem_a_st"] = b.sem("a_st")
    ic1 = b.emit("vector", lambda e: e.memset(t["a_phq"][:, 2, :], 1.0))
    ic2 = b.emit("vector", lambda e: e.memset(t["a_phk"][:, 273:384], 0.0))
    ic2 = b.emit("vector", lambda e: e.memset(t["a_phk"][:, 272:273], 1.0))
    t["a_phq_free"] = ic1
    t["a_phk_free"] = ic2


def build_l0_attn_program():
    nc = bass.Bass("TRN2", target_bir_lowering=False)
    x = nc.dram_tensor("x", [B_PER_CORE, SEQ, DM], F32, kind="ExternalInput").ap()
    wq = nc.dram_tensor("Wq", [DM, 256], F32, kind="ExternalInput").ap()
    wk = nc.dram_tensor("Wk", [DM, 256], F32, kind="ExternalInput").ap()
    wv = nc.dram_tensor("Wv", [DM, DM], F32, kind="ExternalInput").ap()
    wo = nc.dram_tensor("Wo", [DM, DM], F32, kind="ExternalInput").ap()
    xo = nc.dram_tensor("xo", [B_PER_CORE, SEQ, DM], F32, kind="ExternalOutput").ap()
    hT = nc.dram_tensor("hT_scr", [B_PER_CORE, DM, SEQ], BF16, kind="Internal").ap()
    ys = nc.dram_tensor("y_scr", [B_PER_CORE, SEQ, DM], F32, kind="Internal").ap()
    with ExitStack() as st:
        b = B()
        t = {}
        t["sem_wld"] = b.sem("wld")
        emit_const_loads(b, nc, st, t, ["c_ident", "c_identf", "c_ones", "c_trilkm",
                                        "c_selA", "c_selB", "c_selLow", "c_headmask"])
        emit_weight_load(b, nc, st, t, "a_wq", wq, 8, 256)
        emit_weight_load(b, nc, st, t, "a_wk", wk, 8, 256)
        emit_weight_load(b, nc, st, t, "a_wv", wv, 8, 1024)
        emit_weight_load(b, nc, st, t, "a_wo", wo, 8, 1024)
        alloc_pre(st, nc, b, t, want_hT=True)
        alloc_l0_attn(st, nc, b, t)
        alloc_post(st, nc, b, t)
        pre_done = emit_pre(b, nc, t, x, None, hT, want_h=False, want_hT=True)
        t["a_seq_done"] = pre_done
        slot_done = emit_l0_attn(b, nc, t, hT, ys)
        t["__post_gate"] = slot_done
        t["po_y0_free"] = slot_done
        t["po_y1_free"] = slot_done
        emit_post(b, nc, t, x, ys, xo)
        emit_terminal(b, t)
        b.build(nc, st)
    return nc


def emit_terminal(b, t):
    deps = [Tok(t["sem_po_st0"], t["sem_po_st0"].n),
            Tok(t["sem_po_st1"], t["sem_po_st1"].n)]
    b.emit("vector", lambda e: e.engine_nop(), deps=deps)


def emit_l1_ffn(b, nc, t, h_dram, hT_dram, y_dram, waps):
    init = b.emit("sync", lambda e: e.dma_start(out=y_dram[:, :, :], in_=h_dram[:, :, :]),
                  deps=[t["pre_all_done"]], dma=True, sem=t["sem_m_init"])
    for s in range(B_PER_CORE):
        for ncol in range(SEQ // 512):
            ldh = b.emit("sync", lambda e, s=s, ncol=ncol: e.dma_start(
                out=t["m_hTs"][:, :, :],
                in_=hT_dram[s].rearrange("(k p) c -> p k c", p=128)[:, :, ncol * 512:(ncol + 1) * 512]),
                deps=[t.get("m_hTs_free"), t["pre_all_done"]], dma=True,
                sem=t["sem_m_ldh"])
            mm = None
            for kc in range(8):
                mm = b.emit("tensor", lambda e, kc=kc: e.matmul(
                    t["m_pssc"][0:8, 0:512], t["m_rw"][:, kc, 0:8],
                    t["m_hTs"][:, kc, :], start=(kc == 0), stop=(kc == 7)),
                    deps=[ldh, t["tok_m_rw"], t.get("m_pssc_free")])
            t["m_hTs_free"] = mm
            sc = b.emit("vector", lambda e, s=s, ncol=ncol: e.tensor_scalar(
                t["m_lafm"][0:8, ncol * 512:(ncol + 1) * 512], t["m_pssc"][0:8, 0:512],
                t["m_rb"][0:8, :], t["m_rt"][0:8, :], op0=ALU.add, op1=ALU.mult),
                deps=[mm, t["tok_m_rb"], t.get("m_lafm_free")])
            t["m_pssc_free"] = sc
        la_tok = t["m_pssc_free"]
        def fm2tm(dep0):
            toks = []
            for i in range(NT):
                tr = b.emit("tensor", lambda e, i=i: e.transpose(
                    t["m_pssk"][:, 0:8], t["m_lafm"][0:8, i * 128:(i + 1) * 128],
                    t["c_identf"][0:8, 0:8]),
                    deps=[dep0, t.get("m_pssk_free"), t["tok_c_identf"]])
                cp = b.emit("vector", lambda e, i=i: e.tensor_copy(
                    t["m_latm"][:, i, :], t["m_pssk"][:, 0:8]),
                    deps=[tr, t.get("m_latm_free")])
                t["m_pssk_free"] = cp
                toks.append(cp)
            return toks[-1]

        def tm2fm(dep0):
            toks = []
            for i in range(NT):
                tr = b.emit("tensor", lambda e, i=i: e.transpose(
                    t["m_pssk"][0:8, 0:128], t["m_latm"][:, i, :],
                    t["c_identf"][:, :]),
                    deps=[dep0, t.get("m_pssk_free"), t["tok_c_identf"]])
                cp = b.emit("vector", lambda e, i=i: e.tensor_copy(
                    t["m_lafm"][0:8, i * 128:(i + 1) * 128], t["m_pssk"][0:8, 0:128]),
                    deps=[tr, t.get("m_lafm_free")])
                t["m_pssk_free"] = cp
                toks.append(cp)
            return toks[-1]

        cur = fm2tm(la_tok)
        for it in range(3):
            etoks = []
            for i in range(NT):
                nmx = b.emit("vector", lambda e, i=i: e.reduce_max(
                    t["m_nmx"][:, :], t["m_latm"][:, i, :], axis=AX.X, negate=True),
                    deps=[cur, t.get("m_nmx_free")])
                ex = b.emit("scalar", lambda e, i=i: e.activation(
                    t["m_esc"][:, 0:8], t["m_latm"][:, i, :], AF.Exp,
                    bias=t["m_nmx"][:, :], accum_out=t["m_sum"][:, :]),
                    deps=[nmx])
                t["m_nmx_free"] = ex
                ln = b.emit("scalar", lambda e: e.activation(
                    t["m_lns"][:, :], t["m_sum"][:, :], AF.Ln),
                    deps=[ex, t.get("m_lns_free")])
                ls = b.emit("vector", lambda e: e.tensor_tensor(
                    t["m_lse"][:, :], t["m_lns"][:, :], t["m_nmx"][:, :],
                    op=ALU.subtract), deps=[ln])
                t["m_lns_free"] = ls
                up = b.emit("vector", lambda e, i=i: e.tensor_single_scalar(
                    t["m_latm"][:, i, :], t["m_latm"][:, i, :], t["m_lse"][:, :],
                    op=ALU.subtract), deps=[ls])
                etoks.append(up)
            cur = tm2fm(etoks[-1])
            t["m_latm_free"] = cur
            nmx = b.emit("vector", lambda e: e.reduce_max(
                t["m_nmx8"][0:8, :], t["m_lafm"][0:8, :], axis=AX.X, negate=True),
                deps=[cur, t.get("m_nmx8_free")])
            ex = b.emit("scalar", lambda e: e.activation(
                t["m_esc8"][0:8, 0:SEQ], t["m_lafm"][0:8, :], AF.Exp,
                bias=t["m_nmx8"][0:8, :], accum_out=t["m_sum8"][0:8, :]),
                deps=[nmx])
            t["m_nmx8_free"] = ex
            ln = b.emit("scalar", lambda e: e.activation(
                t["m_lns8"][0:8, :], t["m_sum8"][0:8, :], AF.Ln),
                deps=[ex, t.get("m_lns8_free")])
            ls = b.emit("vector", lambda e: e.tensor_tensor(
                t["m_lse8"][0:8, :], t["m_lns8"][0:8, :], t["m_nmx8"][0:8, :],
                op=ALU.subtract), deps=[ln])
            t["m_lns8_free"] = ls
            tok_up = b.emit("vector", lambda e: e.tensor_single_scalar(
                t["m_lafm"][0:8, :], t["m_lafm"][0:8, :], t["m_lse8"][0:8, :],
                op=ALU.subtract), deps=[ls])
            t["m_lafm_free"] = tok_up
            cur = fm2tm(tok_up)
        for i in range(NT):
            g = b.emit("scalar", lambda e, i=i: e.activation(
                t["m_gt"][:, 0:8], t["m_latm"][:, i, :], AF.Exp,
                accum_out=t["m_sum"][:, :]),
                deps=[cur, t.get("m_gt_free"), t.get("m_sum_free")])
            se = b.emit("vector", lambda e: e.tensor_scalar(
                t["m_sum"][:, :], t["m_sum"][:, :], 1e-8, None, op0=ALU.add),
                deps=[g])
            rc = b.emit("vector", lambda e: e.reciprocal(
                t["m_rsum"][:, :], t["m_sum"][:, :]), deps=[se])
            gn = b.emit("vector", lambda e: e.tensor_single_scalar(
                t["m_gt"][:, 0:8], t["m_gt"][:, 0:8], t["m_rsum"][:, :],
                op=ALU.mult), deps=[rc])
            m1 = b.emit("vector", lambda e: e.reduce_max(
                t["m_m1"][:, :], t["m_gt"][:, 0:8], axis=AX.X), deps=[gn])
            eq = b.emit("vector", lambda e: e.tensor_single_scalar(
                t["m_eq"][:, 0:8], t["m_gt"][:, 0:8], t["m_m1"][:, :],
                op=ALU.is_equal), deps=[m1])
            mk = b.emit("vector", lambda e: e.scalar_tensor_tensor(
                t["m_msk"][:, 0:8], t["m_eq"][:, 0:8], -3.0e38, t["m_gt"][:, 0:8],
                op0=ALU.mult, op1=ALU.add), deps=[eq])
            m2 = b.emit("vector", lambda e: e.reduce_max(
                t["m_m2"][:, :], t["m_msk"][:, 0:8], axis=AX.X), deps=[mk])
            ge = b.emit("vector", lambda e: e.tensor_single_scalar(
                t["m_ge"][:, 0:8], t["m_gt"][:, 0:8], t["m_m2"][:, :],
                op=ALU.is_ge), deps=[m2])
            dn = b.emit("vector", lambda e: e.tensor_tensor(
                t["m_dn"][:, :], t["m_m1"][:, :], t["m_m2"][:, :], op=ALU.add),
                deps=[m2])
            dn = b.emit("vector", lambda e: e.tensor_scalar(
                t["m_dn"][:, :], t["m_dn"][:, :], 1e-8, None, op0=ALU.add),
                deps=[dn])
            rdn = b.emit("vector", lambda e: e.reciprocal(
                t["m_rdn"][:, :], t["m_dn"][:, :]), deps=[dn])
            w1 = b.emit("vector", lambda e: e.tensor_tensor(
                t["m_ge"][:, 0:8], t["m_ge"][:, 0:8], t["m_gt"][:, 0:8],
                op=ALU.mult), deps=[rdn])
            t["m_gt_free"] = w1
            t["m_sum_free"] = rc
            wv = b.emit("vector", lambda e, s=s, i=i: e.tensor_single_scalar(
                t["m_w"][:, s * NT + i, :], t["m_ge"][:, 0:8], t["m_rdn"][:, :],
                op=ALU.mult), deps=[w1, t.get("m_w_free")])
            t["m_wtok", s, i] = wv
        t["m_latm_free"] = t["m_wtok", s, NT - 1]
        t["m_lafm_free"] = t["m_wtok", s, NT - 1]

    for e_i in range(8):
        wld = []
        for wnm, src, kch, ncl in (("m_wg", waps["wg"], 8, 2048),
                                   ("m_wp", waps["wp"], 8, 2048),
                                   ("m_wo", waps["wo"], 16, 1024)):
            wld.append(b.emit("gpsimd", lambda e, wnm=wnm, src=src, e_i=e_i: e.dma_start(
                out=t[wnm][:, :, :],
                in_=src[e_i].rearrange("(k p) n -> p k n", p=128)),
                deps=[t.get("m_wfree")], dma=True, sem=t["sem_m_wld"]))
        wtok = wld[-1]
        for s in range(B_PER_CORE):
            for blk in range(SEQ // 512):
                ldh = b.emit("sync", lambda e, s=s, blk=blk: e.dma_start(
                    out=t["m_hTs"][:, :, :],
                    in_=hT_dram[s].rearrange("(k p) c -> p k c", p=128)[:, :, blk * 512:(blk + 1) * 512]),
                    deps=[t.get("m_hTs_free"), t["pre_all_done"]], dma=True,
                    sem=t["sem_m_ldh"])
                for mc in range(16):
                    ga = None
                    for kc in range(8):
                        ga = b.emit("tensor", lambda e, mc=mc, kc=kc: e.matmul(
                            t["m_psA"][:, :], t["m_wg"][:, kc, mc * 128:(mc + 1) * 128],
                            t["m_hTs"][:, kc, :],
                            start=(kc == 0), stop=(kc == 7)),
                            deps=[ldh, wtok, t.get("m_psA_free")])
                    pa = None
                    for kc in range(8):
                        pa = b.emit("tensor", lambda e, mc=mc, kc=kc: e.matmul(
                            t["m_psB"][:, :], t["m_wp"][:, kc, mc * 128:(mc + 1) * 128],
                            t["m_hTs"][:, kc, :],
                            start=(kc == 0), stop=(kc == 7)),
                            deps=[ldh, wtok, t.get("m_psB_free")])
                    rl = b.emit("scalar", lambda e: e.activation(
                        t["m_gsb"][:, :], t["m_psA"][:, :], AF.Relu),
                        deps=[ga, t.get("m_gsb_free")])
                    t["m_psA_free"] = rl
                    gp = b.emit("vector", lambda e, mc=mc: e.tensor_tensor(
                        t["m_gp"][:, mc, :], t["m_gsb"][:, :], t["m_psB"][:, :],
                        op=ALU.mult), deps=[rl, pa, t.get("m_gp_free")])
                    t["m_psB_free"] = gp
                    t["m_gsb_free"] = gp
                    t["m_gptok", mc] = gp
                for tk in range(4):
                    tile = blk * 4 + tk
                    for ncol in range(2):
                        wo_mm = None
                        for kc in range(16):
                            wo_mm = b.emit("tensor", lambda e, tk=tk, ncol=ncol, kc=kc: e.matmul(
                                t["m_psW"][:, :], t["m_gp"][:, kc, tk * 128:(tk + 1) * 128],
                                t["m_wo"][:, kc, ncol * 512:(ncol + 1) * 512],
                                start=(kc == 0), stop=(kc == 15)),
                                deps=[t["m_gptok", kc], wtok, t.get("m_psW_free")])
                        ev = b.emit("vector", lambda e, ncol=ncol, s=s, tile=tile, e_i=e_i: e.tensor_single_scalar(
                            t["m_part"][:, ncol * 512:(ncol + 1) * 512], t["m_psW"][:, :],
                            t["m_w"][:, s * NT + tile, e_i:e_i + 1], op=ALU.mult),
                            deps=[wo_mm, t["m_wtok", s, tile], t.get("m_part_free")])
                        t["m_psW_free"] = ev
                    acc = b.emit("gpsimd", lambda e, s=s, tile=tile: e.dma_start(
                        out=y_dram[s, tile * 128:(tile + 1) * 128, :],
                        in_=t["m_part"][:, :], accum_op=ALU.add),
                        deps=[t["m_psW_free"], init], dma=True, sem=t["sem_m_acc"])
                    t["m_part_free"] = acc
                t["m_gp_free"] = wo_mm
                t["m_hTs_free"] = pa
        t["m_wfree"] = wo_mm
    return Tok(t["sem_m_acc"], t["sem_m_acc"].n)


def alloc_l1_ffn(st, nc, b, t):
    sb = lambda nm, shape, dt: st.enter_context(nc.sbuf_tensor(nm, shape, dt))
    ps = lambda nm, shape, dt: st.enter_context(nc.psum_tensor(nm, shape, dt))
    t["m_hTs"] = sb("m_hTs", [128, 8, 512], BF16)
    t["m_rw"] = sb("m_rw", [128, 8, 8], BF16)
    t["m_rb"] = sb("m_rb", [8, 1], F32)
    t["m_rt"] = sb("m_rt", [8, 1], F32)
    t["m_lafm"] = sb("m_lafm", [8, SEQ], F32)
    t["m_latm"] = sb("m_latm", [128, NT, 8], F32)
    t["m_nmx"] = sb("m_nmx", [128, 1], F32)
    t["m_esc"] = sb("m_esc", [128, 8], F32)
    t["m_sum"] = sb("m_sum", [128, 1], F32)
    t["m_lns"] = sb("m_lns", [128, 1], F32)
    t["m_lse"] = sb("m_lse", [128, 1], F32)
    t["m_nmx8"] = sb("m_nmx8", [8, 1], F32)
    t["m_esc8"] = sb("m_esc8", [8, SEQ], F32)
    t["m_sum8"] = sb("m_sum8", [8, 1], F32)
    t["m_lns8"] = sb("m_lns8", [8, 1], F32)
    t["m_lse8"] = sb("m_lse8", [8, 1], F32)
    t["m_gt"] = sb("m_gt", [128, 8], F32)
    t["m_rsum"] = sb("m_rsum", [128, 1], F32)
    t["m_m1"] = sb("m_m1", [128, 1], F32)
    t["m_m2"] = sb("m_m2", [128, 1], F32)
    t["m_eq"] = sb("m_eq", [128, 8], F32)
    t["m_msk"] = sb("m_msk", [128, 8], F32)
    t["m_ge"] = sb("m_ge", [128, 8], F32)
    t["m_dn"] = sb("m_dn", [128, 1], F32)
    t["m_rdn"] = sb("m_rdn", [128, 1], F32)
    t["m_w"] = sb("m_w", [128, B_PER_CORE * NT, 8], F32)
    t["m_wg"] = sb("m_wg", [128, 8, 2048], BF16)
    t["m_wp"] = sb("m_wp", [128, 8, 2048], BF16)
    t["m_wo"] = sb("m_wo", [128, 16, 1024], BF16)
    t["m_gsb"] = sb("m_gsb", [128, 512], F32)
    t["m_gp"] = sb("m_gp", [128, 16, 512], BF16)
    t["m_part"] = sb("m_part", [128, 1024], F32)
    t["m_pssc"] = ps("m_pssc", [128, 512], F32)
    t["m_pssk"] = ps("m_pssk", [128, 128], F32)
    t["m_psA"] = ps("m_psA", [128, 512], F32)
    t["m_psB"] = ps("m_psB", [128, 512], F32)
    t["m_psW"] = ps("m_psW", [128, 512], F32)
    t["sem_m_init"] = b.sem("m_init")
    t["sem_m_ldh"] = b.sem("m_ldh")
    t["sem_m_wld"] = b.sem("m_wld")
    t["sem_m_acc"] = b.sem("m_acc")


def emit_l1_ffn_router_consts(b, nc, st, t, rw_ap, rb_ap, temp_ap):
    t["tok_m_rw"] = b.emit("gpsimd", lambda e: e.dma_start(
        out=t["m_rw"][:, :, :], in_=rw_ap.rearrange("(k p) e -> p k e", p=128)),
        dma=True, sem=t["sem_m_wld"])
    t["tok_m_rb"] = b.emit("gpsimd", lambda e: e.dma_start(
        out=t["m_rb"][0:8, :], in_=rb_ap.unsqueeze(1)),
        dma=True, sem=t["sem_m_wld"])
    ldt = b.emit("gpsimd", lambda e: e.dma_start(
        out=t["m_rt"][0:8, :], in_=temp_ap.unsqueeze(0).partition_broadcast(8)),
        dma=True, sem=t["sem_m_wld"])
    tm = b.emit("vector", lambda e: e.tensor_scalar_max(
        t["m_rt"][0:8, :], t["m_rt"][0:8, :], 0.1), deps=[ldt])
    rc = b.emit("vector", lambda e: e.reciprocal(
        t["m_rt"][0:8, :], t["m_rt"][0:8, :]), deps=[tm])
    t["tok_m_rb"] = rc


def build_l1_ffn_program():
    nc = bass.Bass("TRN2", target_bir_lowering=False)
    x = nc.dram_tensor("x", [B_PER_CORE, SEQ, DM], F32, kind="ExternalInput").ap()
    rw = nc.dram_tensor("router_wT", [DM, 8], F32, kind="ExternalInput").ap()
    rb = nc.dram_tensor("router_b", [8], F32, kind="ExternalInput").ap()
    temp = nc.dram_tensor("temperature", [1], F32, kind="ExternalInput").ap()
    wg = nc.dram_tensor("wg", [8, DM, 2048], F32, kind="ExternalInput").ap()
    wp = nc.dram_tensor("wp", [8, DM, 2048], F32, kind="ExternalInput").ap()
    wo = nc.dram_tensor("wo", [8, 2048, DM], F32, kind="ExternalInput").ap()
    xo = nc.dram_tensor("xo", [B_PER_CORE, SEQ, DM], F32, kind="ExternalOutput").ap()
    h = nc.dram_tensor("h_scr", [B_PER_CORE, SEQ, DM], F32, kind="Internal").ap()
    hT = nc.dram_tensor("hT_scr", [B_PER_CORE, DM, SEQ], BF16, kind="Internal").ap()
    ys = nc.dram_tensor("y_scr", [B_PER_CORE, SEQ, DM], F32, kind="Internal").ap()
    with ExitStack() as st:
        b = B()
        t = {}
        t["sem_wld"] = b.sem("wld")
        emit_const_loads(b, nc, st, t, ["c_ident", "c_identf"])
        alloc_pre(st, nc, b, t, want_hT=True)
        alloc_l1_ffn(st, nc, b, t)
        alloc_post(st, nc, b, t)
        emit_l1_ffn_router_consts(b, nc, st, t, rw, rb, temp)
        pre_done = emit_pre(b, nc, t, x, h, hT, want_h=True, want_hT=True)
        t["pre_all_done"] = pre_done
        slot_done = emit_l1_ffn(b, nc, t, h, hT, ys, {"wg": wg, "wp": wp, "wo": wo})
        t["po_y0_free"] = slot_done
        t["po_y1_free"] = slot_done
        emit_post(b, nc, t, x, ys, xo)
        emit_terminal(b, t)
        b.build(nc, st)
    return nc



def mamba_consts():
    rep8 = np.zeros((128, 16, 128), dtype=np.float32)
    g8 = np.zeros((128, 16, 128), dtype=np.float32)
    msel = np.zeros((128, 2, 128), dtype=np.float32)
    for j in range(16):
        for r in range(128):
            dd, s = r // 16, r % 16
            rep8[8 * j + dd, j, r] = 1.0
            g8[r, j, 8 * j + dd] = 1.0
    for r in range(128):
        s = r % 16
        msel[64 + s, 0, r] = 1.0
        msel[80 + s, 1, r] = 1.0
    return {"c_rep8": rep8, "c_g8": g8, "c_msel": msel}


CONST_SPECS.update({
    "c_rep8": ([128, 16, 128], BF16),
    "c_g8": ([128, 16, 128], BF16),
    "c_msel": ([128, 2, 128], BF16),
})


def emit_softplus(b, t, out_ap, in_ap, scratch_ap, deps):
    ab = b.emit("scalar", lambda e: e.activation(
        scratch_ap, in_ap, AF.Abs), deps=deps)
    ex = b.emit("scalar", lambda e: e.activation(
        scratch_ap, scratch_ap, AF.Exp, scale=-1.0), deps=[ab])
    ln = b.emit("scalar", lambda e: e.activation(
        scratch_ap, scratch_ap, AF.Ln, bias=t["c_onesf"][:, :]), deps=[ex])
    mx = b.emit("vector", lambda e: e.tensor_scalar_max(out_ap, in_ap, 0.0),
                deps=list(deps) + [ab])
    ad = b.emit("vector", lambda e: e.tensor_tensor(
        out_ap, out_ap, scratch_ap, op=ALU.add), deps=[mx, ln])
    return ad


CONST_SPECS["c_onesf"] = ([128, 1], F32)


def emit_l1_attn_mamba(b, nc, st, t, hT_src, waps, mo_dst, z_scr, xc_scr,
                       ymz_scr, flip):
    pfx = "md"
    ldh = b.emit("sync", lambda e: e.dma_start(
        out=t["q_hT"][:, :, :], in_=hT_src.rearrange("(k p) c -> p k c", p=128)),
        deps=[t.get("q_hT_free")], dma=True, sem=t["sem_q_ld"])
    xdbc_mm = None
    for cc in range(16):
        pr = cc % 2
        lw = b.emit("gpsimd", lambda e, cc=cc, pr=pr: e.dma_start(
            out=t["q_win%d" % pr][:, :, :],
            in_=waps["Win"].rearrange("(k p) n -> p k n", p=128)[:, :, cc * 128:(cc + 1) * 128]),
            deps=[t.get("q_win%d_free" % pr)], dma=True, sem=t["sem_q_w%d" % pr])
        lwz = b.emit("gpsimd", lambda e, cc=cc, pr=pr: e.dma_start(
            out=t["q_winz%d" % pr][:, :, :],
            in_=waps["Win"].rearrange("(k p) n -> p k n", p=128)[:, :, 2048 + cc * 128:2048 + (cc + 1) * 128]),
            deps=[t.get("q_winz%d_free" % pr)], dma=True, sem=t["sem_q_w%d" % pr])
        for n4 in range(SEQ // 512):
            xmm = None
            for kc in range(8):
                xmm = b.emit("tensor", lambda e, kc=kc, n4=n4, pr=pr: e.matmul(
                    t["q_psx"][:, :], t["q_win%d" % pr][:, kc, :],
                    t["q_hT"][:, kc, n4 * 512:(n4 + 1) * 512],
                    start=(kc == 0), stop=(kc == 7)),
                    deps=[ldh, lw, t.get("q_psx_free")])
            ev = b.emit("vector", lambda e, n4=n4: e.tensor_copy(
                t["q_xi"][:, n4 * 512:(n4 + 1) * 512], t["q_psx"][:, :]),
                deps=[xmm, t.get("q_xi_free")])
            t["q_psx_free"] = ev
            zmm = None
            for kc in range(8):
                zmm = b.emit("tensor", lambda e, kc=kc, n4=n4, pr=pr: e.matmul(
                    t["q_psx"][:, :], t["q_winz%d" % pr][:, kc, :],
                    t["q_hT"][:, kc, n4 * 512:(n4 + 1) * 512],
                    start=(kc == 0), stop=(kc == 7)),
                    deps=[ldh, lwz, t.get("q_psx_free")])
            evz = b.emit("vector", lambda e, n4=n4: e.tensor_copy(
                t["q_zbf"][:, n4 * 512:(n4 + 1) * 512], t["q_psx"][:, :]),
                deps=[zmm, t.get("q_zbf_free")])
            t["q_psx_free"] = evz
        t["q_win%d_free" % pr] = xmm
        t["q_winz%d_free" % pr] = zmm
        stz = b.emit("sync", lambda e, cc=cc: e.dma_start(
            out=z_scr[cc], in_=t["q_zbf"][:, :]),
            deps=[t["q_psx_free"]], dma=True, sem=t["sem_q_stz"])
        t["q_zbf_free"] = stz
        cw = t[pfx + "_cw"]
        c1 = b.emit("vector", lambda e, cc=cc: e.tensor_single_scalar(
            t["q_t1"][:, :], t["q_xi"][:, :], cw[:, cc, 3:4], op=ALU.mult),
            deps=[t["q_psx_free"], t.get("q_t1_free"), t["tok_" + pfx + "_cw"]])
        c2 = b.emit("vector", lambda e, cc=cc: e.scalar_tensor_tensor(
            t["q_t1"][:, 1:SEQ], t["q_xi"][:, 0:SEQ - 1], cw[:, cc, 2:3],
            t["q_t1"][:, 1:SEQ], op0=ALU.mult, op1=ALU.add), deps=[c1])
        c3 = b.emit("vector", lambda e, cc=cc: e.scalar_tensor_tensor(
            t["q_t1"][:, 2:SEQ], t["q_xi"][:, 0:SEQ - 2], cw[:, cc, 1:2],
            t["q_t1"][:, 2:SEQ], op0=ALU.mult, op1=ALU.add), deps=[c2])
        c4 = b.emit("vector", lambda e, cc=cc: e.scalar_tensor_tensor(
            t["q_t1"][:, 3:SEQ], t["q_xi"][:, 0:SEQ - 3], cw[:, cc, 0:1],
            t["q_t1"][:, 3:SEQ], op0=ALU.mult, op1=ALU.add), deps=[c3])
        t["q_xi_free"] = c4
        sil = b.emit("scalar", lambda e, cc=cc: e.activation(
            t["q_xc"][:, :], t["q_t1"][:, :], AF.Silu,
            bias=t[pfx + "_cb"][:, cc:cc + 1]),
            deps=[c4, t.get("q_xc_free"), t["tok_" + pfx + "_cb"]])
        t["q_t1_free"] = sil
        stc = b.emit("sync", lambda e, cc=cc: e.dma_start(
            out=xc_scr[cc], in_=t["q_xc"][:, :]),
            deps=[sil], dma=True, sem=t["sem_q_stc"])
        for n4 in range(SEQ // 512):
            xdbc_mm = b.emit("tensor", lambda e, cc=cc, n4=n4: e.matmul(
                t["q_psd"][0:96, n4, :], t[pfx + "_wx"][:, cc, 0:96],
                t["q_xc"][:, n4 * 512:(n4 + 1) * 512],
                start=(cc == 0), stop=(cc == 15)),
                deps=[sil, t["tok_" + pfx + "_wx"], t.get("q_psd_free")])
        t["q_xc_free"] = [xdbc_mm, stc]
        t["q_xcst", cc] = stc
    t["q_hT_free"] = xdbc_mm
    dbf = b.emit("vector", lambda e: e.tensor_copy(
        t["q_xdbcb"][0:96, :], t["q_psd"][0:96, 0:SEQ // 512, :].rearrange("p a b -> p (a b)")),
        deps=[xdbc_mm, t.get("q_xdbcb_free")])
    t["q_psd_free"] = dbf
    t["q_psy_free"] = dbf
    brep = []
    for which in range(2):
        for n4 in range(SEQ // 512):
            mmr = b.emit("tensor", lambda e, which=which, n4=n4: e.matmul(
                t["q_psx"][:, :], t["c_msel"][0:96, which, :],
                t["q_xdbcb"][0:96, n4 * 512:(n4 + 1) * 512],
                start=True, stop=True),
                deps=[dbf, t.get("q_psx_free"), t["tok_c_msel"]])
            ev = b.emit("vector", lambda e, which=which, n4=n4: e.tensor_copy(
                t["q_bc"][:, which, n4 * 512:(n4 + 1) * 512], t["q_psx"][:, :]),
                deps=[mmr, t.get("q_bc_free")])
            t["q_psx_free"] = ev
            brep.append(ev)
    brep_tok = brep[-1]
    for cc in range(16):
        ldc = b.emit("sync", lambda e, cc=cc: e.dma_start(
            out=t["q_xc"][:, :], in_=xc_scr[cc]),
            deps=[t.get("q_xc_free"), t["q_xcst", cc]], dma=True, sem=t["sem_q_ldc"])
        wub = dtb = None
        for n4 in range(SEQ // 512):
            dmm = b.emit("tensor", lambda e, cc=cc, n4=n4: e.matmul(
                t["q_psx"][:, :], t[pfx + "_wdt"][0:64, cc * 128:(cc + 1) * 128],
                t["q_xdbcb"][0:64, n4 * 512:(n4 + 1) * 512],
                start=True, stop=True),
                deps=[dbf, t["tok_" + pfx + "_wdt"], t.get("q_psx_free")])
            bd = b.emit("vector", lambda e, cc=cc, n4=n4: e.tensor_single_scalar(
                t["q_t1"][:, 0:512], t["q_psx"][:, :],
                t[pfx + "_bdt"][:, cc:cc + 1], op=ALU.add),
                deps=[dmm, t.get("q_t1_free"), t["tok_" + pfx + "_bdt"]])
            t["q_psx_free"] = bd
            sp = emit_softplus(b, t, t["q_t1"][:, 0:512], t["q_t1"][:, 0:512],
                               t["q_xi"][:, 0:512], [bd, t.get("q_xi_free")])
            wub = b.emit("vector", lambda e, n4=n4: e.tensor_tensor(
                t["q_wub"][:, n4 * 512:(n4 + 1) * 512], t["q_t1"][:, 0:512],
                t["q_xc"][:, n4 * 512:(n4 + 1) * 512], op=ALU.mult),
                deps=[sp, ldc, t.get("q_wub_free")])
            dtb = b.emit("vector", lambda e, n4=n4: e.tensor_copy(
                t["q_dtb"][:, n4 * 512:(n4 + 1) * 512], t["q_t1"][:, 0:512]),
                deps=[sp, t.get("q_dtb_free")])
            t["q_t1_free"] = dtb
            t["q_xi_free"] = sp
        ymm = None
        for j in range(16):
            for n4 in range(SEQ // 512):
                r1 = b.emit("tensor", lambda e, j=j, n4=n4: e.matmul(
                    t["q_psr"][:, 0, :], t["c_rep8"][:, j, :],
                    t["q_dtb"][:, n4 * 512:(n4 + 1) * 512], start=True, stop=True),
                    deps=[dtb, t.get("q_psr0_free"), t["tok_c_rep8"]])
                da = b.emit("scalar", lambda e, j=j, n4=n4, cc=cc: e.activation(
                    t["q_dA"][:, n4 * 512:(n4 + 1) * 512], t["q_psr"][:, 0, :],
                    AF.Exp, scale=t[pfx + "_A"][:, cc * 16 + j:cc * 16 + j + 1]),
                    deps=[r1, t.get("q_dA_free"), t["tok_" + pfx + "_A"]])
                t["q_psr0_free"] = da
                r2 = b.emit("tensor", lambda e, j=j, n4=n4: e.matmul(
                    t["q_psr"][:, 1, :], t["c_rep8"][:, j, :],
                    t["q_wub"][:, n4 * 512:(n4 + 1) * 512], start=True, stop=True),
                    deps=[wub, t.get("q_psr1_free")])
                d1 = b.emit("vector", lambda e, j=j, n4=n4: e.tensor_tensor(
                    t["q_d1"][:, n4 * 512:(n4 + 1) * 512], t["q_psr"][:, 1, :],
                    t["q_bc"][:, 0, n4 * 512:(n4 + 1) * 512], op=ALU.mult),
                    deps=[r2, brep_tok, t.get("q_d1_free")])
                t["q_psr1_free"] = d1
            sc = b.emit("vector", lambda e: e.tensor_tensor_scan(
                t["q_h"][:, :], t["q_dA"][:, :], t["q_d1"][:, :], 0.0,
                op0=ALU.mult, op1=ALU.add),
                deps=[t["q_psr0_free"], t["q_psr1_free"], t.get("q_h_free")])
            t["q_dA_free"] = sc
            t["q_d1_free"] = sc
            pp = b.emit("vector", lambda e: e.tensor_tensor(
                t["q_P"][:, :], t["q_h"][:, :], t["q_bc"][:, 1, :], op=ALU.mult),
                deps=[sc, t.get("q_P_free")])
            t["q_h_free"] = pp
            for n4 in range(SEQ // 512):
                ymm = b.emit("tensor", lambda e, j=j, n4=n4: e.matmul(
                    t["q_psy"][:, n4, :], t["c_g8"][:, j, :],
                    t["q_P"][:, n4 * 512:(n4 + 1) * 512],
                    start=(j == 0), stop=(j == 15)),
                    deps=[pp, t["tok_c_g8"], t.get("q_psy_free")])
            t["q_P_free"] = ymm
        ldz = b.emit("sync", lambda e, cc=cc: e.dma_start(
            out=t["q_zbf"][:, :], in_=z_scr[cc]),
            deps=[t.get("q_zbf_free")], dma=True, sem=t["sem_q_ldz"])
        for n4 in range(SEQ // 512):
            ym = b.emit("vector", lambda e, cc=cc, n4=n4: e.scalar_tensor_tensor(
                t["q_t1"][:, n4 * 512:(n4 + 1) * 512],
                t["q_xc"][:, n4 * 512:(n4 + 1) * 512],
                t[pfx + "_D"][:, cc:cc + 1],
                t["q_psy"][:, n4, :], op0=ALU.mult, op1=ALU.add),
                deps=[ymm, t.get("q_t1_free"), t["tok_" + pfx + "_D"]])
        t["q_psy_free"] = ym
        sz = b.emit("scalar", lambda e: e.activation(
            t["q_zbf"][:, :], t["q_zbf"][:, :], AF.Silu), deps=[ldz])
        if flip:
            mz = b.emit("vector", lambda e: e.tensor_tensor(
                t["q_t1"][:, :], t["q_t1"][:, :], t["q_zbf"][:, :], op=ALU.mult),
                deps=[ym, sz])
            ymz = b.emit("vector", lambda e: e.tensor_copy(
                t["q_xc"][:, :], t["q_t1"][:, ::-1]), deps=[mz, ym])
        else:
            ymz = b.emit("vector", lambda e: e.tensor_tensor(
                t["q_xc"][:, :], t["q_t1"][:, :], t["q_zbf"][:, :], op=ALU.mult),
                deps=[ym, sz])
        t["q_zbf_free"] = ymz
        t["q_t1_free"] = ymz
        sty = b.emit("sync", lambda e, cc=cc: e.dma_start(
            out=ymz_scr[cc], in_=t["q_xc"][:, :]),
            deps=[ymz], dma=True, sem=t["sem_q_sty"])
        t["q_xc_free"] = sty
        t["q_ymzst", cc] = sty
    lwo = b.emit("gpsimd", lambda e: e.dma_start(
        out=t["q_wout"][:, :, :],
        in_=waps["Wout"].rearrange("(k p) n -> p k n", p=128)),
        deps=[t.get("q_wout_free")], dma=True, sem=t["sem_q_wo"])
    for ti in range(NT):
        tj = (NT - 1 - ti) if flip else ti
        ldy = b.emit("sync", lambda e, tj=tj: e.dma_start(
            out=t["q_ymzT"][:, :, :],
            in_=ymz_scr.rearrange("c p t -> p c t")[:, :, tj * 128:(tj + 1) * 128]),
            deps=[t.get("q_ymzT_free"),
                  [t["q_ymzst", cc] for cc in range(16)]],
            dma=True, sem=t["sem_q_ldt"])
        for ncol in range(2):
            wmm = None
            for kc in range(16):
                wmm = b.emit("tensor", lambda e, kc=kc, ncol=ncol: e.matmul(
                    t["q_psx"][:, :], t["q_ymzT"][:, kc, :],
                    t["q_wout"][:, kc, ncol * 512:(ncol + 1) * 512],
                    start=(kc == 0), stop=(kc == 15)),
                    deps=[ldy, lwo, t.get("q_psx_free")])
            ev = b.emit("vector", lambda e, ncol=ncol: e.tensor_copy(
                t["q_mo"][:, ncol * 512:(ncol + 1) * 512], t["q_psx"][:, :]),
                deps=[wmm, t.get("q_mo_free")])
            t["q_psx_free"] = ev
        t["q_ymzT_free"] = wmm
        stm = b.emit("sync", lambda e, ti=ti: e.dma_start(
            out=mo_dst[ti * 128:(ti + 1) * 128, :], in_=t["q_mo"][:, :]),
            deps=[t["q_psx_free"]], dma=True, sem=t["sem_q_stm"])
        t["q_mo_free"] = stm
    t["q_wout_free"] = wmm
    t["q_bc_free"] = t["q_psy_free"]
    t["q_xdbcb_free"] = t["q_psy_free"]
    return Tok(t["sem_q_stm"], t["sem_q_stm"].n)


def alloc_l1_attn(st, nc, b, t):
    sb = lambda nm, shape, dt: st.enter_context(nc.sbuf_tensor(nm, shape, dt))
    ps = lambda nm, shape, dt: st.enter_context(nc.psum_tensor(nm, shape, dt))
    t["q_hT"] = sb("q_hT", [128, 8, SEQ], BF16)
    for pr in range(2):
        t["q_win%d" % pr] = sb("q_win%d" % pr, [128, 8, 128], BF16)
        t["q_winz%d" % pr] = sb("q_winz%d" % pr, [128, 8, 128], BF16)
        t["sem_q_w%d" % pr] = b.sem("q_w%d" % pr)
    t["q_xi"] = sb("q_xi", [128, SEQ], F32)
    t["q_zbf"] = sb("q_zbf", [128, SEQ], BF16)
    t["q_t1"] = sb("q_t1", [128, SEQ], F32)
    t["q_xc"] = sb("q_xc", [128, SEQ], BF16)
    t["q_xdbcb"] = sb("q_xdbcb", [128, SEQ], BF16)
    t["q_bc"] = sb("q_bc", [128, 2, SEQ], BF16)
    t["q_dtb"] = sb("q_dtb", [128, SEQ], BF16)
    t["q_wub"] = sb("q_wub", [128, SEQ], BF16)
    t["q_dA"] = sb("q_dA", [128, SEQ], F32)
    t["q_d1"] = sb("q_d1", [128, SEQ], F32)
    t["q_h"] = sb("q_h", [128, SEQ], F32)
    t["q_P"] = sb("q_P", [128, SEQ], BF16)
    t["q_ymzT"] = t["q_dtb"].ap().rearrange("p (a b) -> p a b", a=16) if SEQ >= 2048 else sb("q_ymzT", [128, 16, 128], BF16)
    t["q_wout"] = sb("q_wout", [128, 16, 1024], BF16)
    t["q_mo"] = sb("q_mo", [128, 1024], F32)
    t["q_psx"] = ps("q_psx", [128, 512], F32)
    t["q_ps4"] = ps("q_ps4", [128, 4, 512], F32)
    t["q_psr"] = ps("q_psr", [128, 2, 512], F32)
    t["q_psd"] = t["q_ps4"]
    t["q_psy"] = t["q_ps4"]
    t["sem_q_ld"] = b.sem("q_ld")
    t["sem_q_ldc"] = b.sem("q_ldc")
    t["sem_q_ldz"] = b.sem("q_ldz")
    t["sem_q_ldt"] = b.sem("q_ldt")
    t["sem_q_wo"] = b.sem("q_wo")
    t["sem_q_stz"] = b.sem("q_stz")
    t["sem_q_stc"] = b.sem("q_stc")
    t["sem_q_sty"] = b.sem("q_sty")
    t["sem_q_stm"] = b.sem("q_stm")
    t["sem_q_sth1"] = b.sem("q_sth1")


def alloc_l1_attn_dirbufs(st, nc, b, t):
    sb = lambda nm, shape, dt: st.enter_context(nc.sbuf_tensor(nm, shape, dt))
    t["md_cw"] = sb("md_cw", [128, 16, 4], F32)
    t["md_cb"] = sb("md_cb", [128, 16], F32)
    t["md_wx"] = sb("md_wx", [128, 16, 96], BF16)
    t["md_wdt"] = sb("md_wdt", [64, 2048], BF16)
    t["md_bdt"] = sb("md_bdt", [128, 16], F32)
    t["md_D"] = sb("md_D", [128, 16], F32)
    t["md_A"] = sb("md_A", [128, 256], F32)
    t["sem_md"] = b.sem("md")


def emit_l1_attn_dir_consts(b, t, waps, prev_pass_done):
    dep = [prev_pass_done, t.get("md_busy")]
    for nm, key, rr in (("md_cw", "convw", "(c p) j -> p c j"),
                        ("md_wx", "Wx", "(c p) n -> p c n")):
        b.emit("gpsimd", lambda e, nm=nm, key=key, rr=rr: e.dma_start(
            out=t[nm][:, :, :], in_=waps[key].rearrange(rr, p=128)),
            deps=dep, dma=True, sem=t["sem_md"])
    for nm, key in (("md_cb", "convb_r"), ("md_bdt", "bdt_r"), ("md_D", "D_r")):
        b.emit("gpsimd", lambda e, nm=nm, key=key: e.dma_start(
            out=t[nm][:, :], in_=waps[key]), deps=dep, dma=True, sem=t["sem_md"])
    b.emit("gpsimd", lambda e: e.dma_start(
        out=t["md_wdt"][0:64, :], in_=waps["Wdt"]), deps=dep, dma=True,
        sem=t["sem_md"])
    la = b.emit("gpsimd", lambda e: e.dma_start(
        out=t["md_A"][:, :], in_=waps["Alog_r"]), deps=dep, dma=True,
        sem=t["sem_md"])
    full = Tok(t["sem_md"], t["sem_md"].n)
    e1 = b.emit("scalar", lambda e: e.activation(
        t["md_A"][:, :], t["md_A"][:, :], AF.Exp), deps=[full])
    e2 = b.emit("vector", lambda e: e.tensor_scalar_mul(
        t["md_A"][:, :], t["md_A"][:, :], -1.0), deps=[e1])
    for nm in ("md_cw", "md_cb", "md_wx", "md_wdt", "md_bdt", "md_D"):
        t["tok_" + nm[3:] if False else "tok_md_" + nm[3:]] = full
    t["tok_md_cw"] = full
    t["tok_md_cb"] = full
    t["tok_md_wx"] = full
    t["tok_md_wdt"] = full
    t["tok_md_bdt"] = full
    t["tok_md_D"] = full
    t["tok_md_A"] = e2


def emit_l1_attn_combine(b, nc, t, h_dram, mo_scr, ycomb_dram, yT_done_cb):
    last = None
    if "v_psT_free" not in t:
        t["v_psT_free"] = t.get("pre_psT_free")
    for s in range(B_PER_CORE):
        for i in range(NT):
            ldf = b.emit("sync", lambda e, s=s, i=i: e.dma_start(
                out=t["v_f"][:, :], in_=mo_scr[s, 0, i * 128:(i + 1) * 128, :]),
                deps=[t.get("v_f_free"), t["mo_done", s, 0]], dma=True,
                sem=t["sem_v_ld"])
            ldb = b.emit("sync", lambda e, s=s, i=i: e.dma_start(
                out=t["v_b"][:, :], in_=mo_scr[s, 1, i * 128:(i + 1) * 128, :]),
                deps=[t.get("v_b_free"), t["mo_done", s, 1]], dma=True,
                sem=t["sem_v_ldb"])
            ldh = b.emit("sync", lambda e, s=s, i=i: e.dma_start(
                out=t["v_h"][:, :], in_=h_dram[s, i * 128:(i + 1) * 128, :]),
                deps=[t.get("v_h_free")], dma=True, sem=t["sem_v_ldh"])
            a1 = b.emit("vector", lambda e: e.tensor_tensor(
                t["v_f"][:, :], t["v_f"][:, :], t["v_h"][:, :], op=ALU.add),
                deps=[ldf, ldh])
            l1 = emit_ln(b, t, "po", t["v_f"][:, :], t["v_f"][:, :], DM,
                         [a1], g_ap=t["v_fwnw"][:, :], bia_ap=t["v_fwnb"][:, :])
            a2 = b.emit("vector", lambda e: e.tensor_tensor(
                t["v_b"][:, :], t["v_b"][:, :], t["v_h"][:, :], op=ALU.add),
                deps=[ldb, ldh])
            t["v_h_free"] = a2
            l2 = emit_ln(b, t, "po", t["v_b"][:, :], t["v_b"][:, :], DM,
                         [a2], g_ap=t["v_fwnw"][:, :], bia_ap=t["v_fwnb"][:, :])
            yy = b.emit("vector", lambda e: e.tensor_tensor(
                t["v_f"][:, :], t["v_f"][:, :], t["v_b"][:, :], op=ALU.add),
                deps=[l1, l2])
            t["v_b_free"] = yy
            styc = b.emit("sync", lambda e, s=s, i=i: e.dma_start(
                out=ycomb_dram[s, i * 128:(i + 1) * 128, :], in_=t["v_f"][:, :]),
                deps=[yy], dma=True, sem=t["sem_v_stc"])
            ybf = b.emit("vector", lambda e: e.tensor_copy(
                t["v_ybf"][:, :], t["v_f"][:, :]), deps=[yy, t.get("v_ybf_free")])
            t["v_f_free"] = styc
            cps = []
            for k in range(8):
                tr = b.emit("tensor", lambda e, k=k: e.transpose(
                    t["pre_psT"][:, :], t["v_ybf"][:, k * 128:(k + 1) * 128],
                    t["c_ident"][:, :]),
                    deps=[ybf, t.get("v_psT_free"), t["tok_c_ident"]])
                cp = b.emit("vector", lambda e, k=k: e.tensor_copy(
                    t["v_yTt"][:, k, :], t["pre_psT"][:, :]),
                    deps=[tr, t.get("v_yTt_free")])
                t["v_psT_free"] = cp
                cps.append(cp)
            styt = b.emit("sync", lambda e, s=s, i=i: e.dma_start(
                out=yT_done_cb[s].rearrange("(k p) c -> p k c", p=128)[:, :, i * 128:(i + 1) * 128],
                in_=t["v_yTt"][:, :, :]),
                deps=[cps[-1]], dma=True, sem=t["sem_v_styt"])
            t["v_yTt_free"] = styt
            t["v_ybf_free"] = cps[-1]
            last = styt
    return [Tok(t["sem_v_styt"], t["sem_v_styt"].n),
            Tok(t["sem_v_stc"], t["sem_v_stc"].n)], None


def alloc_l1_attn_combine(st, nc, b, t, w_fwn, b_fwn, w_fin, b_fin, _unused, ffb2):
    sb = lambda nm, shape, dt: st.enter_context(nc.sbuf_tensor(nm, shape, dt))
    t["v_fwnw"] = sb("v_fwnw", [128, DM], BF16)
    t["v_fwnb"] = sb("v_fwnb", [128, DM], BF16)
    t["v_finw"] = sb("v_finw", [128, DM], BF16)
    t["v_finb"] = sb("v_finb", [128, DM], BF16)
    t["v_b2"] = sb("v_b2", [128, DM], BF16)
    t["v_f"] = t["q_dA"][:, 0:DM] if ("q_dA" in t and SEQ >= 2 * DM) else sb("v_f", [128, DM], F32)
    t["v_b"] = t["q_dA"][:, DM:2 * DM] if ("q_dA" in t and SEQ >= 2 * DM) else sb("v_b", [128, DM], F32)
    t["v_h"] = t["q_h"][:, 0:DM] if ("q_h" in t and SEQ >= 2 * DM) else sb("v_h", [128, DM], F32)
    t["v_ybf"] = t["q_P"][:, 0:DM] if ("q_P" in t and SEQ >= 2 * DM) else sb("v_ybf", [128, DM], BF16)
    t["v_yTt"] = sb("v_yTt", [128, 8, 128], BF16)
    for nm, ap in (("v_fwnw", w_fwn), ("v_fwnb", b_fwn), ("v_finw", w_fin),
                   ("v_finb", b_fin), ("v_b2", ffb2)):
        t["tok_" + nm] = b.emit("gpsimd", lambda e, nm=nm, ap=ap: e.dma_start(
            out=t[nm][:, :], in_=ap.unsqueeze(0).partition_broadcast(128)),
            dma=True, sem=t["sem_wld"])
    t["sem_v_ld"] = b.sem("v_ld")
    t["sem_v_ldb"] = b.sem("v_ldb")
    t["sem_v_ldh"] = b.sem("v_ldh")
    t["sem_v_st"] = b.sem("v_st")
    t["sem_v_stc"] = b.sem("v_stc")
    t["sem_v_styt"] = b.sem("v_styt")


def emit_l1_attn_ff(b, nc, t, w1_ap, w2_ap, ycomb_dram, y_dram, yT_toks, h1_scr,
                    yT_scr):
    for s in range(B_PER_CORE):
        ldyT = b.emit("sync", lambda e, s=s: e.dma_start(
            out=t["q_hT"][:, :, :],
            in_=yT_scr[s].rearrange("(k p) c -> p k c", p=128)),
            deps=[t.get("q_hT_free"), yT_toks], dma=True, sem=t["sem_q_ld"])
        for mc in range(16):
            pr = mc % 2
            lw = b.emit("gpsimd", lambda e, mc=mc, pr=pr: e.dma_start(
                out=t["q_win%d" % pr][:, :, :],
                in_=w1_ap.rearrange("(k p) n -> p k n", p=128)[:, :, mc * 128:(mc + 1) * 128]),
                deps=[t.get("q_win%d_free" % pr)], dma=True, sem=t["sem_q_w%d" % pr])
            for n4 in range(SEQ // 512):
                mm = None
                for kc in range(8):
                    mm = b.emit("tensor", lambda e, mc=mc, kc=kc, n4=n4, pr=pr: e.matmul(
                        t["q_psx"][:, :], t["q_win%d" % pr][:, kc, :],
                        t["q_hT"][:, kc, n4 * 512:(n4 + 1) * 512],
                        start=(kc == 0), stop=(kc == 7)),
                        deps=[lw, ldyT, t.get("q_psx_free")])
                rl = b.emit("vector", lambda e, mc=mc, n4=n4: e.tensor_scalar(
                    t["q_xc"][:, n4 * 512:(n4 + 1) * 512], t["q_psx"][:, :],
                    t["v_b1r"][:, mc:mc + 1], 0.0, op0=ALU.add, op1=ALU.max),
                    deps=[mm, t.get("q_xc_free"), t["tok_v_b1r"]])
                t["q_psx_free"] = rl
            sth1 = b.emit("sync", lambda e, mc=mc, s=s: e.dma_start(
                out=h1_scr[mc], in_=t["q_xc"][:, :]),
                deps=[t["q_psx_free"]], dma=True, sem=t["sem_q_sth1"])
            t["q_xc_free"] = sth1
            t["h1st", mc] = sth1
            t["q_win%d_free" % pr] = mm
        lw2 = b.emit("gpsimd", lambda e: e.dma_start(
            out=t["q_wout"][:, :, :],
            in_=w2_ap.rearrange("(k p) n -> p k n", p=128)),
            deps=[t.get("q_wout_free")], dma=True, sem=t["sem_q_wo"])
        for i in range(NT):
            ldh1 = b.emit("sync", lambda e, i=i: e.dma_start(
                out=t["q_ymzT"][:, :, :],
                in_=h1_scr.rearrange("c p t -> p c t")[:, :, i * 128:(i + 1) * 128]),
                deps=[t.get("q_ymzT_free"), [t["h1st", mc] for mc in range(16)]],
                dma=True, sem=t["sem_q_ldt"])
            for ncol in range(2):
                mm = None
                for kc in range(16):
                    mm = b.emit("tensor", lambda e, i=i, ncol=ncol, kc=kc: e.matmul(
                        t["q_psx"][:, :], t["q_ymzT"][:, kc, :],
                        t["q_wout"][:, kc, ncol * 512:(ncol + 1) * 512],
                        start=(kc == 0), stop=(kc == 15)),
                        deps=[t["q_psx_free"], lw2, ldh1])
                ev = b.emit("vector", lambda e, ncol=ncol: e.tensor_copy(
                    t["q_mo"][:, ncol * 512:(ncol + 1) * 512], t["q_psx"][:, :]),
                    deps=[mm, t.get("q_mo_free")])
                t["q_psx_free"] = ev
            ldy = b.emit("sync", lambda e, s=s, i=i: e.dma_start(
                out=t["v_h"][:, :], in_=ycomb_dram[s, i * 128:(i + 1) * 128, :]),
                deps=[t.get("v_h_free"), yT_toks], dma=True, sem=t["sem_v_ldh"])
            ad = b.emit("vector", lambda e: e.tensor_tensor(
                t["q_mo"][:, :], t["q_mo"][:, :], t["v_b2"][:, :], op=ALU.add),
                deps=[t["q_psx_free"], t["tok_v_b2"]])
            ad2 = b.emit("vector", lambda e: e.tensor_tensor(
                t["q_mo"][:, :], t["q_mo"][:, :], t["v_h"][:, :], op=ALU.add),
                deps=[ad, ldy])
            t["v_h_free"] = ad2
            ln = emit_ln(b, t, "po", t["q_mo"][:, :], t["q_mo"][:, :], DM,
                         [ad2], g_ap=t["v_finw"][:, :], bia_ap=t["v_finb"][:, :])
            stm = b.emit("sync", lambda e, s=s, i=i: e.dma_start(
                out=y_dram[s, i * 128:(i + 1) * 128, :], in_=t["q_mo"][:, :]),
                deps=[ln], dma=True, sem=t["sem_v_st"])
            t["q_mo_free"] = stm
            t["q_ymzT_free"] = mm
        t["q_wout_free"] = mm
        t["q_hT_free"] = mm
    return Tok(t["sem_v_st"], t["sem_v_st"].n)


def build_l1_attn_program():
    nc = bass.Bass("TRN2", target_bir_lowering=False)
    x = nc.dram_tensor("x", [B_PER_CORE, SEQ, DM], F32, kind="ExternalInput").ap()
    w = {}
    for d in ("fw", "bw"):
        w[d] = {
            "Win": nc.dram_tensor(d + "_Win", [DM, 4096], F32, kind="ExternalInput").ap(),
            "convw": nc.dram_tensor(d + "_convw", [2048, 4], F32, kind="ExternalInput").ap(),
            "convb_r": nc.dram_tensor(d + "_convb_r", [128, 16], F32, kind="ExternalInput").ap(),
            "Wx": nc.dram_tensor(d + "_Wx", [2048, 96], F32, kind="ExternalInput").ap(),
            "Wdt": nc.dram_tensor(d + "_Wdt", [64, 2048], F32, kind="ExternalInput").ap(),
            "bdt_r": nc.dram_tensor(d + "_bdt_r", [128, 16], F32, kind="ExternalInput").ap(),
            "Alog_r": nc.dram_tensor(d + "_Alog_r", [128, 256], F32, kind="ExternalInput").ap(),
            "D_r": nc.dram_tensor(d + "_D_r", [128, 16], F32, kind="ExternalInput").ap(),
            "Wout": nc.dram_tensor(d + "_Wout", [2048, DM], F32, kind="ExternalInput").ap(),
        }
    fwn_w = nc.dram_tensor("fwn_w", [DM], F32, kind="ExternalInput").ap()
    fwn_b = nc.dram_tensor("fwn_b", [DM], F32, kind="ExternalInput").ap()
    fin_w = nc.dram_tensor("fin_w", [DM], F32, kind="ExternalInput").ap()
    fin_b = nc.dram_tensor("fin_b", [DM], F32, kind="ExternalInput").ap()
    ff_W1 = nc.dram_tensor("ff_W1", [DM, 2048], F32, kind="ExternalInput").ap()
    ff_b1r = nc.dram_tensor("ff_b1_r", [128, 16], F32, kind="ExternalInput").ap()
    ff_W2 = nc.dram_tensor("ff_W2", [2048, DM], F32, kind="ExternalInput").ap()
    ff_b2 = nc.dram_tensor("ff_b2", [DM], F32, kind="ExternalInput").ap()
    xo = nc.dram_tensor("xo", [B_PER_CORE, SEQ, DM], F32, kind="ExternalOutput").ap()
    h = nc.dram_tensor("h_scr", [B_PER_CORE, SEQ, DM], F32, kind="Internal").ap()
    hT = nc.dram_tensor("hT_scr", [B_PER_CORE, DM, SEQ], BF16, kind="Internal").ap()
    hTf = nc.dram_tensor("hTf_scr", [B_PER_CORE, DM, SEQ], BF16, kind="Internal").ap()
    z_scr = nc.dram_tensor("z_scr", [16, 128, SEQ], BF16, kind="Internal").ap()
    xc_scr = nc.dram_tensor("xc_scr", [16, 128, SEQ], BF16, kind="Internal").ap()
    ymz_scr = nc.dram_tensor("ymz_scr", [16, 128, SEQ], BF16, kind="Internal").ap()
    mo_scr = nc.dram_tensor("mo_scr", [B_PER_CORE, 2, SEQ, DM], F32, kind="Internal").ap()
    ycomb = nc.dram_tensor("ycomb_scr", [B_PER_CORE, SEQ, DM], F32, kind="Internal").ap()
    yT_scr = nc.dram_tensor("yT_scr", [B_PER_CORE, DM, SEQ], BF16, kind="Internal").ap()
    h1_scr = nc.dram_tensor("h1_scr", [16, 128, SEQ], BF16, kind="Internal").ap()
    ys = nc.dram_tensor("y_scr", [B_PER_CORE, SEQ, DM], F32, kind="Internal").ap()
    with ExitStack() as st:
        b = B()
        t = {}
        t["sem_wld"] = b.sem("wld")
        emit_const_loads(b, nc, st, t, ["c_ident", "c_identf", "c_rep8", "c_g8",
                                        "c_msel", "c_onesf"])
        alloc_pre(st, nc, b, t, want_hT=True, want_hTf=True)
        alloc_l1_attn(st, nc, b, t)
        alloc_l1_attn_combine(st, nc, b, t, fwn_w, fwn_b, fin_w, fin_b, None, ff_b2)
        t["v_b1r"] = st.enter_context(nc.sbuf_tensor("v_b1r", [128, 16], F32))
        t["tok_v_b1r"] = b.emit("gpsimd", lambda e: e.dma_start(
            out=t["v_b1r"][:, :], in_=ff_b1r), dma=True, sem=t["sem_wld"])
        alloc_post(st, nc, b, t)
        alloc_l1_attn_dirbufs(st, nc, b, t)
        pre_done = emit_pre(b, nc, t, x, h, hT, want_h=True, want_hT=True,
                            hTf_dram=hTf)
        t["q_hT_free"] = pre_done
        prev = None
        for s in range(B_PER_CORE):
            for di, d in enumerate(("fw", "bw")):
                emit_l1_attn_dir_consts(b, t, w[d], prev)
                hsrc = hT[s] if d == "fw" else hTf[s]
                tok = emit_l1_attn_mamba(b, nc, st, t, hsrc, w[d],
                                         mo_scr[s, di], z_scr, xc_scr, ymz_scr,
                                         flip=(d == "bw"))
                t["mo_done", s, di] = tok
                t["md_busy"] = tok
                prev = tok
        yT_tok, _ = emit_l1_attn_combine(b, nc, t, h, mo_scr, ycomb, yT_scr)
        slot_done = emit_l1_attn_ff(b, nc, t, ff_W1, ff_W2, ycomb, ys, yT_tok,
                                    h1_scr, yT_scr)
        t["po_y0_free"] = slot_done
        t["po_y1_free"] = slot_done
        emit_post(b, nc, t, x, ys, xo)
        emit_terminal(b, t)
        b.build(nc, st)
    return nc


WINDOW = 256
FF1 = 2730
NCH = 22
LASTW = FF1 - 21 * 128


def l0f_consts():
    c = {}
    H, DH = 16, 64
    half = np.arange(0, DH, 2, dtype=np.float64)
    inv_freq = 1.0 / 10000.0 ** (half / DH)
    tt = np.arange(SEQ, dtype=np.float64)
    freqs = tt[:, None] * inv_freq[None, :]
    freqs = np.concatenate([freqs, freqs], -1)
    base_scale = (half + 0.4 * DH) / (1.4 * DH)
    power = (tt - SEQ // 2) / (WINDOW // 2)
    scale = base_scale[None, :] ** power[:, None]
    scale = np.concatenate([scale, scale], -1)
    cos, sin = np.cos(freqs), np.sin(freqs)
    inv = 1.0 / scale

    def fm(tab):
        t2 = np.concatenate([tab.T, tab.T], axis=0)
        return np.ascontiguousarray(t2).astype(np.float32)

    def sw(tab):
        return np.concatenate([tab[:, 32:], tab[:, :32]], axis=1)

    c["c_qcos"] = fm(cos * scale)
    c["c_qsinsw"] = fm(sw(sin * scale))
    c["c_kcos"] = fm(cos * inv)
    c["c_ksinsw"] = fm(sw(sin * inv))
    rot = np.zeros((128, 128), dtype=np.float32)
    for hh in range(2):
        o = hh * 64
        for f in range(32):
            rot[o + f, o + 32 + f] = -1.0
            rot[o + 32 + f, o + f] = 1.0
    c["c_rotT"] = np.ascontiguousarray(rot.T)
    s2 = np.zeros((128, 2), dtype=np.float32)
    s2[0:64, 0] = 1.0
    s2[64:128, 1] = 1.0
    c["c_sel2"] = s2
    b2 = np.zeros((2, 128), dtype=np.float32)
    b2[0, 0:64] = 1.0
    b2[1, 64:128] = 1.0
    c["c_bc2"] = b2
    m = np.full((128, 768), -30000.0, dtype=np.float32)
    il = np.arange(128)[:, None]
    jl = np.arange(384)[None, :]
    m[:, 0:384] = np.where((jl >= il) & (jl <= il + 256), 0.0, -30000.0)
    jl = np.arange(128)[None, :]
    m[:, 384:512] = np.where(jl <= il, 0.0, -30000.0)
    jl = np.arange(256)[None, :]
    m[:, 512:768] = np.where((jl <= il + 128) & (jl >= il - 128), 0.0, -30000.0)
    c["c_amask"] = m
    return c


CONST_SPECS.update({
    "c_qcos": ([128, SEQ], BF16), "c_qsinsw": ([128, SEQ], BF16),
    "c_kcos": ([128, SEQ], BF16), "c_ksinsw": ([128, SEQ], BF16),
    "c_rotT": ([128, 128], BF16), "c_sel2": ([128, 2], BF16),
    "c_bc2": ([2, 128], BF16), "c_amask": ([128, 768], BF16),
})


def alloc_l0_ffn(st, nc, b, t):
    sb = lambda nm, shape, dt: st.enter_context(nc.sbuf_tensor(nm, shape, dt))
    ps = lambda nm, shape, dt: st.enter_context(nc.psum_tensor(nm, shape, dt))
    t["f_poolA"] = sb("f_poolA", [128, 32768], BF16)
    t["f_poolB"] = sb("f_poolB", [128, 24576], BF16)
    t["f_kp"] = t["f_poolA"].ap()[:, 0:16384].rearrange("p (a b) -> p a b", a=8)
    t["f_vp"] = t["f_poolA"].ap()[:, 16384:32768].rearrange("p (a b) -> p a b", a=16)
    t["f_W2"] = t["f_poolA"].ap()[:, 0:22528].rearrange("p (a b) -> p a b", a=22)
    t["f_lnxT"] = t["f_poolA"].ap()[:, 22528:30720].rearrange("p (a b) -> p a b", a=8)
    t["f_qp"] = t["f_poolB"].ap()[:, 0:8192].rearrange("p (a b) -> p a b", a=8)
    t["f_sxT"] = t["f_poolB"].ap()[:, 8192:16384].rearrange("p (a b) -> p a b", a=8)
    t["f_wo"] = t["f_poolB"].ap()[:, 16384:24576].rearrange("p (a b) -> p a b", a=8)
    t["f_hp"] = t["f_poolB"].ap()[:, 0:22528].rearrange("p (a b) -> p a b", a=22)
    for pr in range(2):
        t["f_wq%d" % pr] = sb("f_wq%d" % pr, [128, 8, 128], BF16)
        t["f_wg%d" % pr] = sb("f_wg%d" % pr, [128, 8, 128], BF16)
        t["sem_f_w%d" % pr] = b.sem("f_w%d" % pr)
    t["f_wv"] = sb("f_wv", [128, 8, 512], BF16)
    t["f_x"] = sb("f_x", [128, DM], F32)
    t["f_xb"] = sb("f_xb", [128, DM], BF16)
    t["f_q32"] = sb("f_q32", [128, 512], F32)
    t["f_sqb"] = sb("f_sqb", [128, 512], BF16)
    t["f_nsb"] = sb("f_nsb", [2, 512], F32)
    t["f_nbf"] = sb("f_nbf", [2, 512], BF16)
    t["f_qn"] = sb("f_qn", [128, 512], BF16)
    t["f_a"] = sb("f_a", [128, 512], BF16)
    t["f_bb"] = sb("f_bb", [128, 512], BF16)
    t["f_att"] = sb("f_att", [128, 512], BF16)
    t["f_attT"] = sb("f_attT", [128, 3, 128], BF16)
    t["f_nm"] = sb("f_nm", [128, 1], F32)
    t["f_nm8"] = sb("f_nm8", [128, 1], F32)
    t["f_rs"] = sb("f_rs", [128, 1], F32)
    t["f_rr"] = sb("f_rr", [128, 1], F32)
    t["f_ao"] = sb("f_ao", [128, DM], F32)
    t["f_aoT"] = sb("f_aoT", [128, 8, 128], BF16)
    t["f_g32"] = sb("f_g32", [128, 512], F32)
    t["f_lng"] = sb("f_lng", [128, DM], BF16)
    t["f_lnb"] = sb("f_lnb", [128, DM], BF16)
    t["f_b2t"] = sb("f_b2t", [128, DM], BF16)
    t["f_qs"] = sb("f_qs", [128, 6], F32)
    t["f_ks"] = sb("f_ks", [128, 6], F32)
    t["f_b1a"] = sb("f_b1a", [128, 6, NCH], F32)
    t["f_b1g"] = sb("f_b1g", [128, 6, NCH], F32)
    t["f_ps0"] = ps("f_ps0", [128, 512], F32)
    t["f_ps1"] = ps("f_ps1", [128, 512], F32)
    t["f_ps2"] = ps("f_ps2", [128, 512], F32)
    t["f_psn"] = ps("f_psn", [2, 512], F32)
    t["f_pso"] = ps("f_pso", [128, 64], F32)
    t["f_psT"] = ps("f_psT", [128, 128], F32)
    t["sem_f_ld"] = b.sem("f_ld")
    t["sem_f_ldb"] = b.sem("f_ldb")
    t["sem_f_ldm"] = b.sem("f_ldm")
    t["sem_f_ldmb"] = b.sem("f_ldmb")
    t["sem_f_ldx"] = b.sem("f_ldx")
    t["sem_f_st"] = b.sem("f_st")
    t["sem_f_st2"] = b.sem("f_st2")
    t["sem_f_wl"] = b.sem("f_wl")


def emit_shift_load(b, t, buf, sem, dram, s, i, deps, key):
    semA, semB = sem
    toks = []
    toks.append(b.emit("sync", lambda e, s=s, i=i: e.dma_start(
        out=buf[:, 0:512], in_=dram[s, i * 128:(i + 1) * 128, 0:512]),
        deps=deps, dma=True, sem=semA))
    if i == 0:
        z = b.emit("vector", lambda e: e.memset(buf[0:1, 512:DM], 0.0), deps=deps)
        toks.append(b.emit("sync", lambda e, s=s: e.dma_start(
            out=buf[1:128, 512:DM], in_=dram[s, 0:127, 512:DM]),
            deps=deps + [z], dma=True, sem=semB))
        toks.append(z)
    else:
        toks.append(b.emit("sync", lambda e, s=s, i=i: e.dma_start(
            out=buf[:, 512:DM], in_=dram[s, i * 128 - 1:(i + 1) * 128 - 1, 512:DM]),
            deps=deps, dma=True, sem=semB))
    return toks


def emit_l0_ffn_qkprep(b, t, which, hf, sub, ch, lyr, projtok):
    cosn = "c_qcos" if which == "q" else "c_kcos"
    sinn = "c_qsinsw" if which == "q" else "c_ksinsw"
    scl = t["f_qs"] if which == "q" else t["f_ks"]
    t0 = hf * 1024 + sub * 512
    ev = b.emit("scalar", lambda e: e.copy(t["f_q32"][:, :], t["f_ps0"][:, :]),
                deps=[projtok, t.get("f_q32_free")])
    sq = b.emit("scalar", lambda e: e.activation(
        t["f_sqb"][:, :], t["f_q32"][:, :], AF.Square),
        deps=[ev, t.get("f_sqb_free")])
    nmm = b.emit("tensor", lambda e: e.matmul(
        t["f_psn"][0:2, :], t["c_sel2"][:, :], t["f_sqb"][:, :],
        start=True, stop=True),
        deps=[sq, t["tok_c_sel2"], t.get("f_psn_free")])
    t["f_sqb_free"] = nmm
    sr = b.emit("scalar", lambda e: e.activation(
        t["f_nsb"][0:2, :], t["f_psn"][0:2, :], AF.Sqrt),
        deps=[nmm, t.get("f_nsb_free")])
    t["f_psn_free"] = sr
    mx = b.emit("vector", lambda e: e.tensor_scalar_max(
        t["f_nsb"][0:2, :], t["f_nsb"][0:2, :], 1e-12), deps=[sr])
    rc = b.emit("vector", lambda e: e.reciprocal(
        t["f_nsb"][0:2, :], t["f_nsb"][0:2, :]), deps=[mx])
    rc = b.emit("vector", lambda e: e.tensor_copy(
        t["f_nbf"][0:2, :], t["f_nsb"][0:2, :]), deps=[rc, t.get("f_nbf_free")])
    t["f_nsb_free"] = rc
    bcm = b.emit("tensor", lambda e: e.matmul(
        t["f_ps1"][:, :], t["c_bc2"][0:2, :], t["f_nbf"][0:2, :],
        start=True, stop=True),
        deps=[rc, t["tok_c_bc2"], t.get("f_ps1_free")])
    t["f_nbf_free"] = bcm
    qn = b.emit("vector", lambda e: e.tensor_tensor(
        t["f_qn"][:, :], t["f_q32"][:, :], t["f_ps1"][:, :], op=ALU.mult),
        deps=[bcm, t.get("f_qn_free")])
    t["f_ps1_free"] = qn
    t["f_q32_free"] = qn
    aa = b.emit("vector", lambda e, ch=ch, lyr=lyr, t0=t0: e.scalar_tensor_tensor(
        t["f_a"][:, :], t["f_qn"][:, :], scl[:, lyr:lyr + 1],
        t[cosn][:, t0:t0 + 512], op0=ALU.mult, op1=ALU.mult),
        deps=[qn, t.get("f_a_free"), t["tok_" + cosn], t["tok_f_qs"]])
    bbv = b.emit("vector", lambda e, ch=ch, lyr=lyr, t0=t0: e.scalar_tensor_tensor(
        t["f_bb"][:, :], t["f_qn"][:, :], scl[:, lyr:lyr + 1],
        t[sinn][:, t0:t0 + 512], op0=ALU.mult, op1=ALU.mult),
        deps=[qn, t.get("f_bb_free"), t["tok_" + sinn]])
    t["f_qn_free"] = bbv
    m1 = b.emit("tensor", lambda e: e.matmul(
        t["f_ps1"][:, :], t["c_ident"][:, :], t["f_a"][:, :],
        start=True, stop=False),
        deps=[aa, t["tok_c_ident"], t.get("f_ps1_free")])
    m2 = b.emit("tensor", lambda e: e.matmul(
        t["f_ps1"][:, :], t["c_rotT"][:, :], t["f_bb"][:, :],
        start=False, stop=True),
        deps=[bbv, t["tok_c_rotT"]])
    t["f_a_free"] = m2
    t["f_bb_free"] = m2
    dst = t["f_qp"] if which == "q" else t["f_kp"]
    off = (hf * 1024 + sub * 512) if which == "k" else (sub * 512)
    cp = b.emit("vector", lambda e, ch=ch, off=off: e.tensor_copy(
        dst[:, ch, off:off + 512], t["f_ps1"][:, :]),
        deps=[m2, t.get("f_qkdst_free")])
    t["f_ps1_free"] = cp
    return cp


def emit_l0_ffn(b, nc, t, waps, xin, xmid):
    for lyr in range(6):
        lwo = b.emit("gpsimd", lambda e, lyr=lyr: e.dma_start(
            out=t["f_wo"][:, :, :],
            in_=waps["Wo"][lyr].rearrange("(k p) n -> p k n", p=128)),
            deps=[t.get("f_poolB_free")], dma=True, sem=t["sem_f_wl"])
        lg = b.emit("gpsimd", lambda e, lyr=lyr: e.dma_start(
            out=t["f_lng"][:, :],
            in_=waps["ln_g"][lyr].unsqueeze(0).partition_broadcast(128)),
            deps=[t.get("f_lng_free")], dma=True, sem=t["sem_f_wl"])
        lb = b.emit("gpsimd", lambda e, lyr=lyr: e.dma_start(
            out=t["f_lnb"][:, :],
            in_=waps["ln_b"][lyr].unsqueeze(0).partition_broadcast(128)),
            deps=[t.get("f_lnb_free")], dma=True, sem=t["sem_f_wl"])
        lb2 = b.emit("gpsimd", lambda e, lyr=lyr: e.dma_start(
            out=t["f_b2t"][:, :],
            in_=waps["b2"][lyr].unsqueeze(0).partition_broadcast(128)),
            deps=[t.get("f_b2t_free")], dma=True, sem=t["sem_f_wl"])
        t["tok_lvec", lyr] = Tok(t["sem_f_wl"], t["sem_f_wl"].n)
        for s in range(B_PER_CORE):
            for hf in range(SEQ // 1024):
                for it in range(8):
                    i = hf * 8 + it
                    lds = emit_shift_load(b, t, t["f_x"],
                                          (t["sem_f_ld"], t["sem_f_ldb"]), xin,
                                          s, i, [t.get("f_x_free")], None)
                    cb = b.emit("vector", lambda e: e.tensor_copy(
                        t["f_xb"][:, :], t["f_x"][:, :]),
                        deps=lds + [t.get("f_xb_free")])
                    t["f_x_free"] = cb
                    cps = []
                    for k in range(8):
                        tr = b.emit("tensor", lambda e, k=k: e.transpose(
                            t["f_psT"][:, :].bitcast(BF16)[:, 0:128],
                            t["f_xb"][:, k * 128:(k + 1) * 128], t["c_ident"][:, :]),
                            deps=[cb, t.get("f_psT_free"), t["tok_c_ident"]])
                        cp = b.emit("vector", lambda e, k=k, it=it: e.tensor_copy(
                            t["f_sxT"][:, k, it * 128:(it + 1) * 128],
                            t["f_psT"][:, :].bitcast(BF16)[:, 0:128]),
                            deps=[tr, t.get("f_sxT_free")])
                        t["f_psT_free"] = cp
                        cps.append(cp)
                    t["f_xb_free"] = cps[-1]
                    t["f_sxTdone", it] = cps[-1]
                sxT_all = [t["f_sxTdone", it] for it in range(8)]
                for nc2 in range(2):
                    lwv = b.emit("gpsimd", lambda e, lyr=lyr, nc2=nc2: e.dma_start(
                        out=t["f_wv"][:, :, :],
                        in_=waps["Wqkv"][lyr].rearrange("(k p) n -> p k n", p=128)[:, :, 2048 + nc2 * 512:2048 + (nc2 + 1) * 512]),
                        deps=[t.get("f_wv_free")], dma=True, sem=t["sem_f_wl"])
                    mm = None
                    for it in range(8):
                        for kc in range(8):
                            mm = b.emit("tensor", lambda e, it=it, kc=kc: e.matmul(
                                t["f_ps2"][:, :], t["f_sxT"][:, kc, it * 128:(it + 1) * 128],
                                t["f_wv"][:, kc, :], start=(kc == 0), stop=(kc == 7)),
                                deps=[lwv, sxT_all[it], t.get("f_ps2_free")])
                        ev = b.emit("vector", lambda e, it=it, nc2=nc2, hf=hf: e.tensor_copy(
                            t["f_vp"][:, hf * 8 + it, nc2 * 512:(nc2 + 1) * 512],
                            t["f_ps2"][:, :]), deps=[mm, t.get("f_vp_free")])
                        t["f_ps2_free"] = ev
                    t["f_wv_free"] = mm
                for which, base in (("q", 0), ("k", 1024)):
                    for ch in range(8):
                        pr = ch % 2
                        lw = b.emit("gpsimd", lambda e, lyr=lyr, base=base, ch=ch, pr=pr: e.dma_start(
                            out=t["f_wq%d" % pr][:, :, :],
                            in_=waps["Wqkv"][lyr].rearrange("(k p) n -> p k n", p=128)[:, :, base + ch * 128:base + (ch + 1) * 128]),
                            deps=[t.get("f_wq%d_free" % pr)], dma=True,
                            sem=t["sem_f_w%d" % pr])
                        for sub in range(2):
                            mm = None
                            for kc in range(8):
                                mm = b.emit("tensor", lambda e, ch=ch, kc=kc, sub=sub, pr=pr: e.matmul(
                                    t["f_ps0"][:, :], t["f_wq%d" % pr][:, kc, :],
                                    t["f_sxT"][:, kc, sub * 512:(sub + 1) * 512],
                                    start=(kc == 0), stop=(kc == 7)),
                                    deps=[lw, sxT_all[sub * 4 + 3], sxT_all[7],
                                          t.get("f_ps0_free")])
                            cp = emit_l0_ffn_qkprep(b, t, which, hf, sub, ch, lyr, mm)
                            t["f_ps0_free"] = cp
                            t["f_qkprep_done"] = cp
                        t["f_wq%d_free" % pr] = mm
                t["f_sxT_free"] = mm
                for it in range(8):
                    qt = hf * 8 + it
                    nkt = min(qt + 1, 3)
                    for hd in range(16):
                        ch = hd // 2
                        hrow = (hd % 2) * 64
                        if qt == 0:
                            moff, klen = 384, 128
                        elif qt == 1:
                            moff, klen = 512, 256
                        else:
                            moff, klen = 0, 384
                        k0 = (qt - 2) * 128 if qt >= 2 else 0
                        smm = b.emit("tensor", lambda e, ch=ch, hrow=hrow, it=it, k0=k0, klen=klen: e.matmul(
                            t["f_ps1"][:, 0:klen],
                            t["f_qp"][hrow:hrow + 64, ch, it * 128:(it + 1) * 128],
                            t["f_kp"][hrow:hrow + 64, ch, k0:k0 + klen],
                            start=True, stop=False),
                            deps=[t["f_qkprep_done"], t.get("f_ps1_free")])
                        t["f_last_smm"] = smm
                        mmm = b.emit("tensor", lambda e, moff=moff, klen=klen: e.matmul(
                            t["f_ps1"][:, 0:klen], t["c_ident"][:, :],
                            t["c_amask"][:, moff:moff + klen],
                            start=False, stop=True),
                            deps=[t["tok_c_amask"], t["tok_c_ident"]])
                        nm = b.emit("vector", lambda e, klen=klen: e.reduce_max(
                            t["f_nm"][:, :], t["f_ps1"][:, 0:klen], axis=AX.X,
                            negate=True), deps=[mmm])
                        n8 = b.emit("vector", lambda e: e.tensor_scalar_mul(
                            t["f_nm8"][:, :], t["f_nm"][:, :], 8.0), deps=[nm])
                        ex = b.emit("scalar", lambda e, klen=klen: e.activation(
                            t["f_att"][:, 0:klen], t["f_ps1"][:, 0:klen], AF.Exp,
                            bias=t["f_nm8"][:, :], scale=8.0,
                            accum_out=t["f_rs"][:, :]),
                            deps=[n8, t.get("f_att_free")])
                        t["f_ps1_free"] = ex
                        rc = b.emit("vector", lambda e: e.reciprocal(
                            t["f_rr"][:, :], t["f_rs"][:, :]), deps=[ex])
                        cps = []
                        for kt in range(nkt):
                            tr = b.emit("tensor", lambda e, kt=kt, klen=klen: e.transpose(
                                t["f_psT"][:, :].bitcast(BF16)[:, 0:128],
                                t["f_att"][:, kt * 128:(kt + 1) * 128],
                                t["c_ident"][:, :]),
                                deps=[ex, t.get("f_psT_free"), t["tok_c_ident"]])
                            cp = b.emit("vector", lambda e, kt=kt: e.tensor_copy(
                                t["f_attT"][:, kt, :],
                                t["f_psT"][:, :].bitcast(BF16)[:, 0:128]),
                                deps=[tr, t.get("f_attT_free")])
                            t["f_psT_free"] = cp
                            cps.append(cp)
                        t["f_att_free"] = cps[-1]
                        vmm = None
                        for kt in range(nkt):
                            ktile = (qt - nkt + 1 + kt)
                            vmm = b.emit("tensor", lambda e, kt=kt, ktile=ktile, hd=hd, nkt=nkt: e.matmul(
                                t["f_pso"][:, :], t["f_attT"][:, kt, :],
                                t["f_vp"][:, ktile, hd * 64:(hd + 1) * 64],
                                start=(kt == 0), stop=(kt == nkt - 1)),
                                deps=[cps[kt], t.get("f_pso_free")])
                        t["f_attT_free"] = vmm
                        dv = b.emit("vector", lambda e, hd=hd: e.tensor_single_scalar(
                            t["f_ao"][:, hd * 64:(hd + 1) * 64], t["f_pso"][:, :],
                            t["f_rr"][:, :], op=ALU.mult),
                            deps=[vmm, rc, t.get("f_ao_free")])
                        t["f_pso_free"] = dv
                        t["f_aodone", hd] = dv
                    deps_ao = [t["f_aodone", hd] for hd in range(16)]
                    cps = []
                    for k in range(8):
                        tr = b.emit("tensor", lambda e, k=k: e.transpose(
                            t["f_psT"][:, :], t["f_ao"][:, k * 128:(k + 1) * 128],
                            t["c_identf"][:, :]),
                            deps=deps_ao + [t.get("f_psT_free"), t["tok_c_identf"]])
                        cp = b.emit("vector", lambda e, k=k: e.tensor_copy(
                            t["f_aoT"][:, k, :], t["f_psT"][:, :]),
                            deps=[tr, t.get("f_aoT_free")])
                        t["f_psT_free"] = cp
                        cps.append(cp)
                    t["f_ao_free"] = cps[-1]
                    ldx = b.emit("sync", lambda e, s=s, qt=qt: e.dma_start(
                        out=t["f_x"][:, :], in_=xin[s, qt * 128:(qt + 1) * 128, :]),
                        deps=[t.get("f_x_free")], dma=True, sem=t["sem_f_ldx"])
                    t["f_x_free"] = None
                    for ncol in range(2):
                        mm = None
                        for kc in range(8):
                            mm = b.emit("tensor", lambda e, kc=kc, ncol=ncol: e.matmul(
                                t["f_ps2"][:, :], t["f_aoT"][:, kc, :],
                                t["f_wo"][:, kc, ncol * 512:(ncol + 1) * 512],
                                start=(kc == 0), stop=(kc == 7)),
                                deps=[cps[kc], lwo, t.get("f_ps2_free")])
                        ad = b.emit("vector", lambda e, ncol=ncol: e.tensor_tensor(
                            t["f_ao"][:, ncol * 512:(ncol + 1) * 512],
                            t["f_ps2"][:, :], t["f_x"][:, ncol * 512:(ncol + 1) * 512],
                            op=ALU.add), deps=[mm, ldx, cps[-1]])
                        t["f_ps2_free"] = ad
                    t["f_aoT_free"] = mm
                    stm = b.emit("sync", lambda e, s=s, qt=qt: e.dma_start(
                        out=xmid[s, qt * 128:(qt + 1) * 128, :], in_=t["f_ao"][:, :]),
                        deps=[t["f_ps2_free"]], dma=True, sem=t["sem_f_st"])
                    t["f_ao_free"] = stm
                    t["f_x_free"] = stm
                    t["f_xmid_st", s, qt] = stm
                t["f_qkdst_free"] = t["f_last_smm"]
            t["f_vp_free"] = t["f_attT_free"]
        zm = b.emit("vector", lambda e: e.memset(t["f_W2"][:, 21, :], 0.0),
                    deps=[t.get("f_poolA_free"), t.get("f_vp_free"),
                          t.get("f_qkdst_free")])
        lw2a = b.emit("gpsimd", lambda e, lyr=lyr: e.dma_start(
            out=t["f_W2"][:, 0:21, :],
            in_=waps["W2"][lyr][0:2688].rearrange("(k p) n -> p k n", p=128)),
            deps=[zm], dma=True, sem=t["sem_f_wl"])
        lw2b = b.emit("gpsimd", lambda e, lyr=lyr: e.dma_start(
            out=t["f_W2"][0:LASTW, 21, :], in_=waps["W2"][lyr][2688:2730]),
            deps=[zm], dma=True, sem=t["sem_f_wl"])
        w2tok = Tok(t["sem_f_wl"], t["sem_f_wl"].n)
        zm2 = b.emit("vector", lambda e: e.memset(t["f_hp"][:, 21, :], 0.0),
                     deps=[t.get("f_poolB_free")])
        for s in range(B_PER_CORE):
            for hf in range(SEQ // 1024):
                for it in range(8):
                    i = hf * 8 + it
                    lds = emit_shift_load(b, t, t["f_x"],
                                          (t["sem_f_ldm"], t["sem_f_ldmb"]), xmid,
                                          s, i, [t.get("f_x_free"),
                                                 t["f_xmid_st", s, i],
                                                 (t["f_xmid_st", s, i - 1] if i > 0 else None)], None)
                    ln = emit_ln(b, t, "po", t["f_x"][:, :], t["f_x"][:, :], DM,
                                 lds, g_ap=t["f_lng"][:, :], bia_ap=t["f_lnb"][:, :],
                                 dep_extra=[lg, lb])
                    cb = b.emit("vector", lambda e: e.tensor_copy(
                        t["f_xb"][:, :], t["f_x"][:, :]),
                        deps=[ln, t.get("f_xb_free")])
                    t["f_x_free"] = cb
                    cps = []
                    for k in range(8):
                        tr = b.emit("tensor", lambda e, k=k: e.transpose(
                            t["f_psT"][:, :].bitcast(BF16)[:, 0:128],
                            t["f_xb"][:, k * 128:(k + 1) * 128], t["c_ident"][:, :]),
                            deps=[cb, t.get("f_psT_free"), t["tok_c_ident"]])
                        cp = b.emit("vector", lambda e, k=k, it=it: e.tensor_copy(
                            t["f_lnxT"][:, k, it * 128:(it + 1) * 128],
                            t["f_psT"][:, :].bitcast(BF16)[:, 0:128]),
                            deps=[tr, t.get("f_lnxT_free"), w2tok])
                        t["f_psT_free"] = cp
                        cps.append(cp)
                    t["f_xb_free"] = cps[-1]
                    t["f_lnxTdone", it] = cps[-1]
                lnxT_all = [t["f_lnxTdone", it] for it in range(8)]
                for chp in range(NCH):
                    pr = chp % 2
                    wid = 128 if chp < 21 else LASTW
                    lwa = b.emit("gpsimd", lambda e, lyr=lyr, chp=chp, pr=pr, wid=wid: e.dma_start(
                        out=t["f_wq%d" % pr][:, :, 0:wid],
                        in_=waps["W1"][lyr].rearrange("(k p) n -> p k n", p=128)[:, :, chp * 128:chp * 128 + wid]),
                        deps=[t.get("f_wq%d_free" % pr)], dma=True,
                        sem=t["sem_f_w%d" % pr])
                    lwg = b.emit("gpsimd", lambda e, lyr=lyr, chp=chp, pr=pr, wid=wid: e.dma_start(
                        out=t["f_wg%d" % pr][:, :, 0:wid],
                        in_=waps["W1"][lyr].rearrange("(k p) n -> p k n", p=128)[:, :, FF1 + chp * 128:FF1 + chp * 128 + wid]),
                        deps=[t.get("f_wg%d_free" % pr)], dma=True,
                        sem=t["sem_f_w%d" % pr])
                    for sub in range(2):
                        am = None
                        for kc in range(8):
                            am = b.emit("tensor", lambda e, chp=chp, kc=kc, sub=sub, pr=pr, wid=wid: e.matmul(
                                t["f_ps0"][0:wid, :], t["f_wq%d" % pr][:, kc, 0:wid],
                                t["f_lnxT"][:, kc, sub * 512:(sub + 1) * 512],
                                start=(kc == 0), stop=(kc == 7)),
                                deps=[lwa, lnxT_all[sub * 4 + 3], lnxT_all[7],
                                      t.get("f_ps0_free")])
                        gm = None
                        for kc in range(8):
                            gm = b.emit("tensor", lambda e, chp=chp, kc=kc, sub=sub, pr=pr, wid=wid: e.matmul(
                                t["f_ps2"][0:wid, :], t["f_wg%d" % pr][:, kc, 0:wid],
                                t["f_lnxT"][:, kc, sub * 512:(sub + 1) * 512],
                                start=(kc == 0), stop=(kc == 7)),
                                deps=[lwg, lnxT_all[sub * 4 + 3], lnxT_all[7],
                                      t.get("f_ps2_free")])
                        gl = b.emit("scalar", lambda e, chp=chp, lyr=lyr, wid=wid: e.activation(
                            t["f_g32"][0:wid, :], t["f_ps2"][0:wid, :], AF.Gelu,
                            bias=t["f_b1g"][0:wid, lyr, chp:chp + 1]),
                            deps=[gm, t.get("f_g32_free"), t["tok_f_b1a"]])
                        t["f_ps2_free"] = gl
                        hp = b.emit("vector", lambda e, chp=chp, lyr=lyr, sub=sub, wid=wid: e.scalar_tensor_tensor(
                            t["f_hp"][0:wid, chp, sub * 512:(sub + 1) * 512],
                            t["f_ps0"][0:wid, :], t["f_b1a"][0:wid, lyr, chp:chp + 1],
                            t["f_g32"][0:wid, :], op0=ALU.add, op1=ALU.mult),
                            deps=[am, gl, zm2, t.get("f_hp_free")])
                        t["f_ps0_free"] = hp
                        t["f_g32_free"] = hp
                        t["f_hpdone", chp, sub] = hp
                    t["f_wq%d_free" % pr] = am
                    t["f_wg%d_free" % pr] = gm
                for it in range(8):
                    i = hf * 8 + it
                    sub = it // 4
                    hdeps = [t["f_hpdone", chp, sub] for chp in range(NCH)]
                    ldm = b.emit("sync", lambda e, s=s, i=i: e.dma_start(
                        out=t["f_x"][:, :], in_=xmid[s, i * 128:(i + 1) * 128, :]),
                        deps=[t.get("f_x_free"), t["f_xmid_st", s, i]],
                        dma=True, sem=t["sem_f_ldx"])
                    for ncol in range(2):
                        mm = None
                        for chp in range(NCH):
                            mm = b.emit("tensor", lambda e, chp=chp, it=it, ncol=ncol: e.matmul(
                                t["f_ps1"][:, :], t["f_hp"][:, chp, it * 128:(it + 1) * 128],
                                t["f_W2"][:, chp, ncol * 512:(ncol + 1) * 512],
                                start=(chp == 0), stop=(chp == NCH - 1)),
                                deps=hdeps + [w2tok, t.get("f_ps1_free")])
                        ad = b.emit("vector", lambda e, ncol=ncol: e.scalar_tensor_tensor(
                            t["f_ao"][:, ncol * 512:(ncol + 1) * 512],
                            t["f_x"][:, ncol * 512:(ncol + 1) * 512], 1.0,
                            t["f_ps1"][:, :], op0=ALU.mult, op1=ALU.add),
                            deps=[mm, ldm, t.get("f_ao_free"), lb2])
                        ad = b.emit("vector", lambda e, ncol=ncol: e.tensor_tensor(
                            t["f_ao"][:, ncol * 512:(ncol + 1) * 512],
                            t["f_ao"][:, ncol * 512:(ncol + 1) * 512],
                            t["f_b2t"][:, ncol * 512:(ncol + 1) * 512], op=ALU.add),
                            deps=[ad])
                        t["f_ps1_free"] = ad
                    t["f_x_free"] = ad
                    st2 = b.emit("sync", lambda e, s=s, i=i: e.dma_start(
                        out=xin[s, i * 128:(i + 1) * 128, :], in_=t["f_ao"][:, :]),
                        deps=[ad], dma=True, sem=t["sem_f_st2"])
                    t["f_ao_free"] = st2
                    t["f_xin_st", s, i] = st2
                t["f_hp_free"] = mm
                t["f_lnxT_free"] = gm
        st2full = Tok(t["sem_f_st2"], t["sem_f_st2"].n)
        t["f_poolA_free"] = mm
        t["f_poolB_free"] = mm
        t["f_lng_free"] = st2full
        t["f_lnb_free"] = st2full
        t["f_b2t_free"] = st2full
        t["f_x_free"] = [t["f_x_free"], st2full]
    return Tok(t["sem_f_st2"], t["sem_f_st2"].n)


def emit_l0f_launch_consts(b, nc, t, qs_ap, ks_ap, b1a_ap, b1g_ap):
    t["tok_f_qs"] = b.emit("gpsimd", lambda e: e.dma_start(
        out=t["f_qs"][:, :], in_=qs_ap), dma=True, sem=t["sem_wld"])
    t["tok_f_ks"] = b.emit("gpsimd", lambda e: e.dma_start(
        out=t["f_ks"][:, :], in_=ks_ap), dma=True, sem=t["sem_wld"])
    t["tok_f_b1a"] = b.emit("gpsimd", lambda e: e.dma_start(
        out=t["f_b1a"][:, :, :], in_=b1a_ap), dma=True, sem=t["sem_wld"])
    tk = b.emit("gpsimd", lambda e: e.dma_start(
        out=t["f_b1g"][:, :, :], in_=b1g_ap), dma=True, sem=t["sem_wld"])
    t["tok_f_b1a"] = tk


def build_l0_ffn_program():
    nc = bass.Bass("TRN2", target_bir_lowering=False)
    x = nc.dram_tensor("x", [B_PER_CORE, SEQ, DM], F32, kind="ExternalInput").ap()
    wq = nc.dram_tensor("Wqkv", [6, DM, 3072], F32, kind="ExternalInput").ap()
    wo = nc.dram_tensor("Wo", [6, DM, DM], F32, kind="ExternalInput").ap()
    w1 = nc.dram_tensor("W1", [6, DM, 2 * FF1], F32, kind="ExternalInput").ap()
    w2 = nc.dram_tensor("W2", [6, FF1, DM], F32, kind="ExternalInput").ap()
    lng = nc.dram_tensor("ln_g", [6, DM], F32, kind="ExternalInput").ap()
    lnb = nc.dram_tensor("ln_b", [6, DM], F32, kind="ExternalInput").ap()
    b2 = nc.dram_tensor("b2", [6, DM], F32, kind="ExternalInput").ap()
    qs = nc.dram_tensor("qs_r", [128, 6], F32, kind="ExternalInput").ap()
    ks = nc.dram_tensor("ks_r", [128, 6], F32, kind="ExternalInput").ap()
    b1a = nc.dram_tensor("b1a_r", [128, 6, NCH], F32, kind="ExternalInput").ap()
    b1g = nc.dram_tensor("b1g_r", [128, 6, NCH], F32, kind="ExternalInput").ap()
    xo = nc.dram_tensor("xo", [B_PER_CORE, SEQ, DM], F32, kind="ExternalOutput").ap()
    xin = nc.dram_tensor("xin_scr", [B_PER_CORE, SEQ, DM], F32, kind="Internal").ap()
    xmid = nc.dram_tensor("xmid_scr", [B_PER_CORE, SEQ, DM], F32, kind="Internal").ap()
    with ExitStack() as st:
        b = B()
        t = {}
        t["sem_wld"] = b.sem("wld")
        emit_const_loads(b, nc, st, t, ["c_ident", "c_identf", "c_qcos", "c_qsinsw",
                                        "c_kcos", "c_ksinsw", "c_rotT", "c_sel2",
                                        "c_bc2", "c_amask"])
        alloc_pre(st, nc, b, t, want_hT=False)
        alloc_l0_ffn(st, nc, b, t)
        alloc_post(st, nc, b, t)
        emit_l0f_launch_consts(b, nc, t, qs, ks, b1a, b1g)
        pre_done = emit_pre(b, nc, t, x, xin, None, want_h=True, want_hT=False)
        t["f_x_free"] = pre_done
        waps = {"Wqkv": wq, "Wo": wo, "W1": w1, "W2": w2, "ln_g": lng,
                "ln_b": lnb, "b2": b2}
        slot_done = emit_l0_ffn(b, nc, t, waps, xin, xmid)
        t["po_y0_free"] = slot_done
        t["po_y1_free"] = slot_done
        emit_post(b, nc, t, x, xin, xo)
        emit_terminal(b, t)
        b.build(nc, st)
    return nc


def l0f_host_layouts(q_scale, k_scale, b1):
    qs_r = np.ascontiguousarray(np.tile(q_scale.T, (2, 1)))
    ks_r = np.ascontiguousarray(np.tile(k_scale.T, (2, 1)))
    b1a = np.zeros((128, 6, NCH), dtype=np.float32)
    b1g = np.zeros((128, 6, NCH), dtype=np.float32)
    for c in range(NCH):
        w = 128 if c < 21 else LASTW
        b1a[0:w, :, c] = b1[:, c * 128:c * 128 + w].T
        b1g[0:w, :, c] = b1[:, FF1 + c * 128:FF1 + c * 128 + w].T
    return {"qs_r": qs_r, "ks_r": ks_r, "b1a_r": b1a, "b1g_r": b1g}


_PROG_CACHE = {}


def _get_prog(name, builder):
    if name not in _PROG_CACHE:
        _PROG_CACHE[name] = builder()
    return _PROG_CACHE[name]


def kernel(**inp):
    consts = make_consts()
    consts.update(mamba_consts())
    consts.update(l0f_consts())
    consts["c_onesf"] = np.ones((128, 1), dtype=np.float32)

    x = np.ascontiguousarray(inp["L0_attn_hidden_states"], dtype=np.float32)
    shards = [x[2 * c:2 * c + 2] for c in range(N_CORES)]

    def launch(nc, im_common):
        ims = []
        for c in range(N_CORES):
            d = dict(im_common)
            d["x"] = shards[c]
            ims.append(d)
        res = run_bass_kernel_spmd(nc, ims, core_ids=list(range(N_CORES)))
        for c in range(N_CORES):
            shards[c] = np.ascontiguousarray(res.results[c]["xo"])

    ck = lambda names: {k: consts[k] for k in names}

    nc1 = _get_prog("l0a", build_l0_attn_program)
    im = {"Wq": inp["L0_attn_Wq"], "Wk": inp["L0_attn_Wk"],
          "Wv": inp["L0_attn_Wv"], "Wo": inp["L0_attn_Wo"]}
    im.update(ck(["c_ident", "c_identf", "c_ones", "c_trilkm", "c_selA",
                  "c_selB", "c_selLow", "c_headmask"]))
    launch(nc1, im)

    nc2 = _get_prog("l0f", build_l0_ffn_program)
    im = {"Wqkv": inp["L0_ffn_Wqkv"], "Wo": inp["L0_ffn_Wo"],
          "W1": inp["L0_ffn_W1"], "W2": inp["L0_ffn_W2"],
          "ln_g": inp["L0_ffn_ln_g"], "ln_b": inp["L0_ffn_ln_b"],
          "b2": inp["L0_ffn_b2"]}
    im.update(l0f_host_layouts(np.asarray(inp["L0_ffn_q_scale"], dtype=np.float32),
                               np.asarray(inp["L0_ffn_k_scale"], dtype=np.float32),
                               np.asarray(inp["L0_ffn_b1"], dtype=np.float32)))
    im.update(ck(["c_ident", "c_identf", "c_qcos", "c_qsinsw", "c_kcos",
                  "c_ksinsw", "c_rotT", "c_sel2", "c_bc2", "c_amask"]))
    launch(nc2, im)

    nc3 = _get_prog("l1a", build_l1_attn_program)
    im = {}
    for d in ("fw", "bw"):
        p = "L1_attn_" + d + "_"
        im[d + "_Win"] = inp[p + "Win"]
        im[d + "_convw"] = inp[p + "convw"]
        im[d + "_convb_r"] = np.ascontiguousarray(
            np.asarray(inp[p + "convb"], dtype=np.float32).reshape(16, 128).T)
        im[d + "_Wx"] = inp[p + "Wx"]
        im[d + "_Wdt"] = inp[p + "Wdt"]
        im[d + "_bdt_r"] = np.ascontiguousarray(
            np.asarray(inp[p + "bdt"], dtype=np.float32).reshape(16, 128).T)
        im[d + "_Alog_r"] = np.ascontiguousarray(
            np.asarray(inp[p + "Alog"], dtype=np.float32).reshape(256, 8, 16)
            .transpose(1, 2, 0).reshape(128, 256))
        im[d + "_D_r"] = np.ascontiguousarray(
            np.asarray(inp[p + "D"], dtype=np.float32).reshape(16, 128).T)
        im[d + "_Wout"] = inp[p + "Wout"]
    im["fwn_w"] = inp["L1_attn_fwn_w"]
    im["fwn_b"] = inp["L1_attn_fwn_b"]
    im["fin_w"] = inp["L1_attn_fin_w"]
    im["fin_b"] = inp["L1_attn_fin_b"]
    im["ff_W1"] = inp["L1_attn_ff_W1"]
    im["ff_b1_r"] = np.ascontiguousarray(
        np.asarray(inp["L1_attn_ff_b1"], dtype=np.float32).reshape(16, 128).T)
    im["ff_W2"] = inp["L1_attn_ff_W2"]
    im["ff_b2"] = inp["L1_attn_ff_b2"]
    im.update(ck(["c_ident", "c_identf", "c_rep8", "c_g8", "c_msel", "c_onesf"]))
    launch(nc3, im)

    nc4 = _get_prog("l1f", build_l1_ffn_program)
    im = {"router_wT": np.ascontiguousarray(
              np.asarray(inp["L1_ffn_router_w"], dtype=np.float32).T),
          "router_b": inp["L1_ffn_router_b"],
          "temperature": inp["L1_ffn_temperature"],
          "wg": inp["L1_ffn_wg"], "wp": inp["L1_ffn_wp"], "wo": inp["L1_ffn_wo"]}
    im.update(ck(["c_ident", "c_identf"]))
    launch(nc4, im)

    return np.concatenate(shards, axis=0)
